# version 20
# baseline (speedup 1.0000x reference)
import numpy as np
import ml_dtypes
from contextlib import ExitStack
import concourse.bass as bass
import concourse.mybir as mybir
from concourse.bass_utils import run_bass_kernel_spmd

F32 = mybir.dt.float32
BF16 = mybir.dt.bfloat16
AF = mybir.ActivationFunctionType
ALU = mybir.AluOpType
AX = mybir.AxisListType.X

D = 2048
HE = 16
S = 2048
NQ = 1024
HF = 5632
NW = HF * D
SCALE = float(np.float32(1.0) / np.float32(np.sqrt(128.0)))
ENGINES = ("tensor", "vector", "scalar", "sync", "gpsimd")


class Sched:

    def __init__(self, nc, es):
        self.nc = nc
        self.es = es
        self.sem = {n: es.enter_context(nc.semaphore(f"S_{n}")) for n in ENGINES}
        self.cnt = {n: 0 for n in ENGINES}
        self.steps = {n: [] for n in ENGINES}
        self.waited = {n: {} for n in ENGINES}
        self.pending = {n: [] for n in ENGINES}

    def _sem(self, key):
        if key not in self.sem:
            self.sem[key] = self.es.enter_context(self.nc.semaphore(f"S_{key}"))
            self.cnt[key] = 0
        return self.sem[key]

    def step(self, eng, fn, deps=(), dma=False, stream=None, inc=None):
        key = eng
        if dma:
            assert stream is not None, "dma steps must name a stream"
            key = "D_" + stream
            self._sem(key)
        deps = list(deps) + self.pending[eng]
        self.pending[eng] = []
        waits = {}
        for d in deps:
            if d is None:
                continue
            e, v, isd = d
            if v > self.waited[eng].get(e, 0):
                waits[e] = max(waits.get(e, 0), v)
        for e, v in waits.items():
            self.waited[eng][e] = v
        if inc is None:
            inc = 16 if dma else 1
        self.cnt[key] += inc
        self.steps[eng].append((dict(waits), fn, inc, key))
        return (key, self.cnt[key], dma)

    def barrier(self):
        toks = [(e, v, e.startswith("D_")) for e, v in self.cnt.items() if v > 0]
        for e in ENGINES:
            self.pending[e] = list(toks)

    def flush(self):
        nc, sem, steps = self.nc, self.sem, self.steps
        if not any(steps[n] for n in ENGINES):
            return
        with nc.Block() as blk:
            def mk(name):
                def body(eng):
                    for waits, fn, inc, key in steps[name]:
                        for e, v in waits.items():
                            eng.wait_ge(sem[e], v)
                        fn(eng).then_inc(sem[key], inc)
                return body
            for name, reg in (("tensor", blk.tensor), ("vector", blk.vector),
                              ("scalar", blk.scalar), ("sync", blk.sync),
                              ("gpsimd", blk.gpsimd)):
                if steps[name]:
                    reg(mk(name))
        self.steps = {n: [] for n in ENGINES}


class Res:
    def __init__(self):
        self.w = None
        self.r = []

    def rdeps(self):
        return [self.w] if self.w else []

    def wdeps(self):
        return ([self.w] if self.w else []) + self.r

    def wrote(self, tok):
        self.w = tok
        self.r = []

    def read(self, tok):
        self.r.append(tok)


def ln_stats(sc, stat, sqs_v, r_stat, r_sqs, src_ap, src_tok, sq_fd, co=0):
    c0, c1, c2, c3, c4 = co, co + 1, co + 2, co + 3, co + 4
    t1 = sc.step("vector", lambda v: v.reduce_sum(stat[:, c0:c0 + 1], src_ap, axis=AX),
                 deps=[src_tok] + r_stat.wdeps())
    tq = sc.step("scalar", lambda e: e.activation(sqs_v, src_ap, AF.Square, accum_out=stat[:, c1:c1 + 1]),
                 deps=[src_tok] + r_sqs.wdeps() + r_stat.wdeps())
    r_sqs.wrote(tq)
    t3 = sc.step("vector", lambda v: v.tensor_scalar(stat[:, c0:c0 + 1], stat[:, c0:c0 + 1], 1.0 / sq_fd, None, op0=ALU.mult), deps=[t1])
    t4 = sc.step("vector", lambda v: v.tensor_scalar(stat[:, c1:c1 + 1], stat[:, c1:c1 + 1], 1.0 / sq_fd, 1e-5,
                                                     op0=ALU.mult, op1=ALU.add), deps=[tq])
    t5 = sc.step("vector", lambda v: v.tensor_tensor(stat[:, c4:c4 + 1], stat[:, c0:c0 + 1], stat[:, c0:c0 + 1], op=ALU.mult), deps=[t3])
    t6 = sc.step("vector", lambda v: v.tensor_tensor(stat[:, c1:c1 + 1], stat[:, c1:c1 + 1], stat[:, c4:c4 + 1], op=ALU.subtract), deps=[t4, t5])
    t7 = sc.step("scalar", lambda e: e.activation(stat[:, c2:c2 + 1], stat[:, c1:c1 + 1], AF.Sqrt), deps=[t6])
    t8 = sc.step("vector", lambda v: v.reciprocal(stat[:, c2:c2 + 1], stat[:, c2:c2 + 1]), deps=[t7])
    t9 = sc.step("vector", lambda v: v.tensor_scalar(stat[:, c3:c3 + 1], stat[:, c0:c0 + 1], stat[:, c2:c2 + 1], -1.0,
                                                     op0=ALU.mult, op1=ALU.mult), deps=[t8])
    r_stat.wrote(t9)
    return t9


def build_nc():
    nc = bass.Bass("TRN2", target_bir_lowering=False, enable_partition_id=False)
    xb = nc.dram_tensor("xb", [S, D], F32, kind="ExternalInput").ap()
    wqkv_t = nc.dram_tensor("wqkv_t", [HE, 128, 6144], F32, kind="ExternalInput").ap()
    bqkv_pm = nc.dram_tensor("bqkv_pm", [128, 48], F32, kind="ExternalInput").ap()
    wout_t = nc.dram_tensor("wout_t", [16, 128, D], F32, kind="ExternalInput").ap()
    bout_v = nc.dram_tensor("bout_v", [1, D], F32, kind="ExternalInput").ap()
    w1_t = nc.dram_tensor("w1_t", [44, 128, D], F32, kind="ExternalInput").ap()
    w2_t = nc.dram_tensor("w2_t", [44, 128, D], F32, kind="ExternalInput").ap()
    w3_t = nc.dram_tensor("w3_t", [44, 128, D], F32, kind="ExternalInput").ap()
    gsrc = nc.dram_tensor("gsrc", [3, 11, 128, 1024], F32, kind="ExternalInput").ap()
    identw = nc.dram_tensor("identw", [128, 128], BF16, kind="ExternalInput").ap()
    gpart_in = nc.dram_tensor("gpart_in", [1, 8], F32, kind="Internal").ap()
    gpart_out = nc.dram_tensor("gpart_out", [1, 8], F32, kind="Internal").ap()
    out = nc.dram_tensor("out", [NQ, D], F32, kind="ExternalOutput").ap()
    kvh = nc.dram_tensor("kvh", [HE, 2, 128, S], BF16, kind="Internal").ap()
    ln2sp = nc.dram_tensor("ln2sp", [16, 128, NQ], BF16, kind="Internal").ap()
    x1sp = nc.dram_tensor("x1sp", [NQ, D], F32, kind="Internal").ap()
    w1q = nc.dram_tensor("w1q", [D, HF], BF16, kind="Internal").ap()
    w2q = nc.dram_tensor("w2q", [D, HF], BF16, kind="Internal").ap()
    w3q = nc.dram_tensor("w3q", [HF, D], BF16, kind="Internal").ap()
    wq_flat = {"w1": w1q.rearrange("a b -> (a b)"), "w2": w2q.rearrange("a b -> (a b)"),
               "w3": w3q.rearrange("a b -> (a b)")}

    es = ExitStack()
    sc = Sched(nc, es)
    A = es.enter_context

    bqkv = A(nc.sbuf_tensor("bqkv", [128, 48], F32))
    ident = A(nc.sbuf_tensor("ident", [128, 128], BF16))
    ones_b = A(nc.sbuf_tensor("ones_b", [128, 1], BF16))
    ones_f = A(nc.sbuf_tensor("ones_f", [128, 1], F32))
    ones_r = A(nc.sbuf_tensor("ones_r", [1, 128], F32))
    gacc = A(nc.sbuf_tensor("gacc", [128, 3], F32))
    gtmp = A(nc.sbuf_tensor("gtmp", [128, 1], F32))
    tvals = A(nc.sbuf_tensor("tvals", [1, 3], F32))
    gpre = A(nc.sbuf_tensor("gpre", [1, 8], F32))
    tvals8 = A(nc.sbuf_tensor("tvals8", [1, 8], F32))
    trep = A(nc.sbuf_tensor("trep", [128, 3], F32))
    tneg = A(nc.sbuf_tensor("tneg", [128, 3], F32))
    stat = A(nc.sbuf_tensor("stat", [128, 16], F32))
    sqs = A(nc.sbuf_tensor("sqs", [128, D], F32))
    cscale = A(nc.sbuf_tensor("cscale", [128, 1], F32))
    esAT = ExitStack()
    aT = esAT.enter_context(nc.sbuf_tensor("aT", [128, HE * NQ], BF16))

    r_statl = [Res(), Res(), Res()]
    r_sqs = Res()
    r_gacc, r_tval = Res(), Res()
    r_gtmp = Res()
    r_kvh = [[Res(), Res()] for _ in range(HE)]
    r_aT = [Res() for _ in range(HE)]
    r_ln2sp = [Res() for _ in range(8)]
    r_x1sp = [Res() for _ in range(8)]
    r_w1q, r_w2q, r_w3q = Res(), Res(), Res()

    t_bq = sc.step("gpsimd", lambda g: g.dma_start(out=bqkv[:], in_=bqkv_pm[:, :]), dma=True, stream="bq")
    t_id = sc.step("gpsimd", lambda g: g.dma_start(out=ident[:], in_=identw[:, :]), dma=True, stream="id")
    t_o1 = sc.step("vector", lambda v: v.memset(ones_b[:], 1.0))
    t_o2 = sc.step("vector", lambda v: v.memset(ones_f[:], 1.0))
    t_o3 = sc.step("vector", lambda v: v.memset(ones_r[:], 1.0))
    tg0 = sc.step("vector", lambda v: v.memset(gacc[:], 0.0))
    r_gacc.wrote(tg0)
    t_gp0 = sc.step("vector", lambda v: v.memset(gpre[:], 0.0))
    t_cs = sc.step("vector", lambda v: v.memset(cscale[:], SCALE))

    esQ = ExitStack()
    Q_ = esQ.enter_context
    qin = Q_(nc.sbuf_tensor("qin", [128, 2 * D], F32))
    qa = Q_(nc.sbuf_tensor("qa", [128, D], BF16))
    qb = Q_(nc.sbuf_tensor("qb", [128, D], BF16))
    qw = Q_(nc.sbuf_tensor("qw", [128, 2 * D], BF16))
    QT = Q_(nc.sbuf_tensor("QT", [128, HE * NQ], BF16))
    r_qin = [Res(), Res()]
    r_qa, r_qb = Res(), Res()
    r_qw = [Res(), Res()]
    r_QT = [Res() for _ in range(HE)]

    QT_LIST = [("w1", i) for i in range(44)] + [("w2", i) for i in range(44)] + [("w3", i) for i in range(44)]
    GAMMA_LIST = [(m, i) for m in range(3) for i in range(11)]
    WSRC = {"w1": w1_t, "w2": w2_t, "w3": w3_t}
    WCOL = {"w1": 0, "w2": 1, "w3": 2}
    bg = []
    fin = {}

    def mk_gamma(m, i, slot):
        def go():
            qv = qin[:, slot * D:slot * D + 1024]
            td = sc.step("sync", lambda e, m_=m, i_=i, q_=qv: e.dma_start(out=q_, in_=gsrc[m_, i_, :, :]),
                         deps=r_qin[slot].wdeps(), dma=True, stream=f"qin{slot}")
            r_qin[slot].wrote(td)
            tr = sc.step("vector", lambda v, q_=qv: v.reduce_sum(gtmp[:], q_, axis=AX, apply_absolute_value=True),
                         deps=r_qin[slot].rdeps() + r_gtmp.wdeps())
            r_qin[slot].read(tr)
            r_gtmp.wrote(tr)
            ta = sc.step("vector", lambda v, c_=m: v.tensor_tensor(gacc[:, c_:c_ + 1], gacc[:, c_:c_ + 1],
                                                                   gtmp[:], op=ALU.add),
                         deps=[tr] + r_gacc.wdeps())
            r_gtmp.read(ta)
            r_gacc.wrote(ta)
        return go

    for j, (m, i) in enumerate(GAMMA_LIST):
        bg.append(mk_gamma(m, i, j % 2))

    def gamma_finalize():
        ps, r_ps = fin["psum"], fin["res"]
        tm = sc.step("tensor", lambda t: t.matmul(ps[0:1, 0:3], ones_f[:], gacc[:, 0:3], start=True, stop=True),
                     deps=[t_o2] + r_gacc.wdeps() + r_ps.wdeps())
        r_gacc.read(tm)
        r_ps.wrote(tm)
        tc0 = sc.step("vector", lambda v: v.tensor_copy(gpre[:, 0:3], ps[0:1, 0:3]), deps=[tm, t_gp0])
        r_ps.read(tc0)
        tdo = sc.step("gpsimd", lambda g: g.dma_start(out=gpart_in, in_=gpre[:]),
                      deps=[tc0], dma=True, stream="gpo")
        tcc = sc.step("gpsimd", lambda g: g.collective_compute(
            "AllReduce", ALU.add, replica_groups=[list(range(8))],
            ins=[gpart_in], outs=[gpart_out]),
            deps=[tdo], dma=True, stream="cc", inc=1)
        tdi = sc.step("sync", lambda e: e.dma_start(out=tvals8[:], in_=gpart_out),
                      deps=[tcc], dma=True, stream="gpi")
        tc = sc.step("vector", lambda v: v.tensor_scalar(tvals[:, :], tvals8[:, 0:3], 0.5 / NW, 0.5e-8,
                                                         op0=ALU.mult, op1=ALU.add), deps=[tdi])
        tr = sc.step("tensor", lambda t: t.matmul(ps[:, 0:3], ones_r[:], tvals[:, :], start=True, stop=True),
                     deps=[tc, t_o3] + r_ps.wdeps())
        r_ps.wrote(tr)
        tt = sc.step("vector", lambda v: v.tensor_copy(trep[:], ps[:, 0:3]), deps=[tr])
        r_ps.read(tt)
        tn = sc.step("vector", lambda v: v.tensor_scalar(tneg[:], trep[:], -1.0, None, op0=ALU.mult), deps=[tt])
        r_tval.wrote(tn)
    bg.append(gamma_finalize)
    IDX_FIN = len(GAMMA_LIST) + 1

    def mk_quant(m, i, slot):
        def go():
            src = WSRC[m]
            c = WCOL[m]
            qv = qin[:, slot * D:(slot + 1) * D]
            qwv = qw[:, slot * D:(slot + 1) * D]
            td = sc.step("sync", lambda e, s_=src, i_=i, q_=qv: e.dma_start(out=q_, in_=s_[i_, :, :]),
                         deps=r_qin[slot].wdeps(), dma=True, stream=f"qin{slot}")
            r_qin[slot].wrote(td)
            ta = sc.step("vector", lambda v, q_=qv, c_=c: v.tensor_scalar(qa[:], q_, trep[:, c_:c_ + 1], None, op0=ALU.is_gt),
                         deps=r_qin[slot].rdeps() + [r_tval.w] + r_qa.wdeps())
            r_qin[slot].read(ta)
            r_qa.wrote(ta)
            tb = sc.step("vector", lambda v, q_=qv, c_=c: v.tensor_scalar(qb[:], q_, tneg[:, c_:c_ + 1], None, op0=ALU.is_lt),
                         deps=r_qin[slot].rdeps() + [r_tval.w] + r_qb.wdeps())
            r_qin[slot].read(tb)
            r_qb.wrote(tb)
            tw = sc.step("vector", lambda v, qw_=qwv: v.tensor_tensor(qw_, qa[:], qb[:], op=ALU.subtract),
                         deps=[r_qa.w, r_qb.w] + r_qw[slot].wdeps())
            r_qa.read(tw)
            r_qb.read(tw)
            r_qw[slot].wrote(tw)
            dap = wq_flat[m][i * 128 * D:(i + 1) * 128 * D].rearrange("(p c) -> p c", p=128)
            grp = {"w1": r_w1q, "w2": r_w2q, "w3": r_w3q}[m]
            ts_ = sc.step("gpsimd", lambda g, qw_=qwv, d_=dap: g.dma_start(out=d_, in_=qw_),
                          deps=r_qw[slot].rdeps(), dma=True, stream=f"qw{slot}")
            r_qw[slot].read(ts_)
            grp.read(ts_)
        return go

    for j, (m, i) in enumerate(QT_LIST):
        bg.append(mk_quant(m, i, j % 2))

    bg_pos = [0]

    def pump(n, cap=None):
        lim = len(bg) if cap is None else min(cap, len(bg))
        while n > 0 and bg_pos[0] < lim:
            bg[bg_pos[0]]()
            bg_pos[0] += 1
            n -= 1

    esA = ExitStack()
    ln1T = esA.enter_context(nc.sbuf_tensor("ln1T", [128, 16 * S], BF16))
    pk0 = esA.enter_context(nc.psum_tensor("pk0", [128, 2048], F32))
    pk1 = esA.enter_context(nc.psum_tensor("pk1", [128, 1024], F32))
    r_ln1T = Res()
    r_pk0a, r_pk0b, r_pk1, r_pk2 = Res(), Res(), Res(), Res()
    fin["psum"], fin["res"] = pk1, r_pk1

    with (nc.sbuf_tensor("xst", [128, 2 * D], F32) as xst,
          nc.sbuf_tensor("ln1b", [128, 2 * D], BF16) as ln1b,
          nc.psum_tensor("pT", [128, 2048], BF16) as pT):
        r_xst = [Res(), Res()]
        r_ln1b = [Res(), Res()]
        r_pT = Res()
        for t in range(16):
            sl = t % 2
            xv = xst[:, sl * D:(sl + 1) * D]
            lv = ln1b[:, sl * D:(sl + 1) * D]
            tok_x = sc.step("sync", lambda e, t_=t, x_=xv: e.dma_start(out=x_, in_=xb[t_ * 128:(t_ + 1) * 128, :]),
                            deps=r_xst[sl].wdeps(), dma=True, stream=f"xin{sl}")
            r_xst[sl].wrote(tok_x)
            co = 5 * (t % 2)
            rst = r_statl[t % 2]
            tok_s = ln_stats(sc, stat, sqs[:], rst, r_sqs, xv, tok_x, D, co=co)
            tok_n = sc.step("scalar", lambda e, x_=xv, l_=lv, c_=co: e.activation(l_, x_, AF.Identity,
                                                                                  bias=stat[:, c_ + 3:c_ + 4], scale=stat[:, c_ + 2:c_ + 3]),
                            deps=[tok_s] + r_ln1b[sl].wdeps())
            r_xst[sl].read(tok_n)
            rst.read(tok_n)
            r_ln1b[sl].wrote(tok_n)
            toks_tr = []
            for kd in range(16):
                def mk_tr(lv_=lv, kd_=kd):
                    def f(te):
                        return te.transpose(pT[:, kd_ * 128:(kd_ + 1) * 128], lv_[:, kd_ * 128:(kd_ + 1) * 128], ident[:])
                    return f
                dd = [tok_n, t_id] + (r_pT.wdeps() if kd == 0 else [])
                toks_tr.append(sc.step("tensor", mk_tr(), deps=dd))
            r_ln1b[sl].read(toks_tr[-1])
            r_pT.wrote(toks_tr[-1])
            tok_e = sc.step("vector", lambda v, t_=t: v.tensor_copy(
                ln1T[:].rearrange("p (kd s) -> p kd s", kd=16)[:, :, t_ * 128:(t_ + 1) * 128],
                pT[:].rearrange("p (kd c) -> p kd c", kd=16)),
                deps=[toks_tr[-1]] + r_ln1T.rdeps() + ([r_ln1T.w] if r_ln1T.w else []))
            r_pT.read(tok_e)
            r_ln1T.wrote(tok_e)
            pump(5, cap=IDX_FIN)
        sc.flush()
    ln1T_tok = r_ln1T.w

    with nc.sbuf_tensor("wqh", [128, 2 * 6144], BF16) as wqh, \
         nc.sbuf_tensor("ktv", [128, 2 * S], BF16) as ktv, \
         nc.sbuf_tensor("vts", [128, 2 * 1024], BF16) as vts, \
         nc.psum_tensor("pvT", [128, 1024], BF16) as pvT:
        r_wqh = [Res(), Res()]
        r_ktv = [Res(), Res()]
        r_vts = [Res(), Res()]
        r_pvT = Res()
        wq_toks = [None] * HE
        vt_pend = []

        def flush_vt():
            for kvv_, half_, tok_ke_, sl_, h_ in vt_pend:
                toks_vt = []
                for c in range(8):
                    def mk_vt(kvv2=kvv_, half2=half_, c2=c):
                        def f(te):
                            return te.transpose(pvT[:, c2 * 128:(c2 + 1) * 128],
                                                kvv2[:, half2 * 1024 + c2 * 128: half2 * 1024 + c2 * 128 + 128],
                                                ident[:])
                        return f
                    dd = [tok_ke_, t_id] + (r_pvT.wdeps() if c == 0 else [])
                    toks_vt.append(sc.step("tensor", mk_vt(), deps=dd))
                r_ktv[sl_].read(toks_vt[-1])
                r_pvT.wrote(toks_vt[-1])
                tok_vc = sc.step("vector", lambda v, half2=half_: v.tensor_copy(
                    vts[:, half2 * 1024:(half2 + 1) * 1024], pvT[:]),
                    deps=[toks_vt[-1]] + r_vts[half_].wdeps())
                r_pvT.read(tok_vc)
                r_vts[half_].wrote(tok_vc)
                tok_vs = sc.step("gpsimd", lambda g, h2=h_, half2=half_: g.dma_start(
                    out=kvh[h2, 1, :, half2 * 1024:(half2 + 1) * 1024],
                    in_=vts[:, half2 * 1024:(half2 + 1) * 1024]),
                    deps=[tok_vc], dma=True, stream=f"ktv{sl_}")
                r_vts[half_].read(tok_vs)
                r_kvh[h_][1].wrote(tok_vs)
            vt_pend.clear()

        def emit_wqh(h):
            sl2 = h % 2
            wv2 = wqh[:, sl2 * 6144:(sl2 + 1) * 6144]
            wq_toks[h] = sc.step("gpsimd", lambda g, h_=h, w_=wv2: g.dma_start(out=w_, in_=wqkv_t[h_, :, :]),
                                 deps=r_wqh[sl2].wdeps(), dma=True, stream=f"wqh{sl2}")
            r_wqh[sl2].wrote(wq_toks[h])

        emit_wqh(0)
        for h in range(HE):
            sl = h % 2
            wv = wqh[:, sl * 6144:(sl + 1) * 6144]
            if h + 1 < HE:
                emit_wqh(h + 1)
            tok_w = wq_toks[h]

            pq = pk1
            r_pq = r_pk1

            def mk_q(wv_=wv, pq_=pq):
                def f(te):
                    ins = None
                    for kd in range(16):
                        for n in range(2):
                            ins = te.matmul(pq_[:, n * 512:(n + 1) * 512],
                                            wv_[:, kd * 128:kd * 128 + 128],
                                            ln1T[:, kd * S + n * 512: kd * S + n * 512 + 512],
                                            start=(kd == 0), stop=(kd == 15))
                    return ins
                return f
            tok_q = sc.step("tensor", mk_q(), deps=[tok_w, ln1T_tok] + r_pq.wdeps())
            r_wqh[sl].read(tok_q)
            r_pq.wrote(tok_q)
            tok_qe = sc.step("scalar", lambda e, h_=h, pq_=pq: e.activation(QT[:, h_ * NQ:(h_ + 1) * NQ], pq_[:, 0:1024],
                                                                            AF.Identity, bias=bqkv[:, h_:h_ + 1], scale=1.0),
                             deps=[tok_q, t_bq] + r_QT[h].wdeps())
            r_pq.read(tok_qe)
            r_QT[h].wrote(tok_qe)
            flush_vt()

            for j, mcol in ((0, 1), (1, 2)):
                kvv = ktv[:, sl * S:(sl + 1) * S]
                bcol = 16 * mcol + h
                tok_kes = []
                for half in range(2):
                    phalf = [r_pk0a, r_pk0b][half]
                    pap = pk0[:, half * 1024:(half + 1) * 1024]
                    def mk_kv(wv_=wv, mcol_=mcol, half_=half, pap_=pap):
                        def f(te):
                            ins = None
                            for kd in range(16):
                                for n in range(2):
                                    ins = te.matmul(pap_[:, n * 512:(n + 1) * 512],
                                                    wv_[:, mcol_ * 2048 + kd * 128: mcol_ * 2048 + kd * 128 + 128],
                                                    ln1T[:, kd * S + half_ * 1024 + n * 512:
                                                         kd * S + half_ * 1024 + n * 512 + 512],
                                                    start=(kd == 0), stop=(kd == 15))
                            return ins
                        return f
                    tok_kv = sc.step("tensor", mk_kv(), deps=[tok_w, ln1T_tok] + phalf.wdeps())
                    r_wqh[sl].read(tok_kv)
                    phalf.wrote(tok_kv)
                    tok_ke = sc.step("scalar", lambda e, b_=bcol, k_=kvv, h2_=half, p_=pap: e.activation(
                        k_[:, h2_ * 1024:(h2_ + 1) * 1024], p_, AF.Identity, bias=bqkv[:, b_:b_ + 1], scale=1.0),
                        deps=[tok_kv, t_bq] + (r_ktv[sl].wdeps() if half == 0 else []))
                    phalf.read(tok_ke)
                    tok_kes.append(tok_ke)
                    if j == 1:
                        vt_pend.append((kvv, half, tok_ke, sl, h))
                if j == 0:
                    r_ktv[sl].wrote(tok_kes[-1])
                    tok_ks = sc.step("gpsimd", lambda g, h_=h, k_=kvv: g.dma_start(out=kvh[h_, 0, :, :], in_=k_),
                                     deps=tok_kes, dma=True, stream=f"ktv{sl}")
                    r_ktv[sl].read(tok_ks)
                    r_kvh[h][0].wrote(tok_ks)
                else:
                    r_ktv[sl].wrote(tok_kes[-1])
            pump(5)
        flush_vt()
        sc.flush()
    esA.close()

    sc.barrier()
    with (nc.sbuf_tensor("KTb", [128, 2 * S], BF16) as KTb,
          nc.sbuf_tensor("Vh", [128, 2 * S], BF16) as Vh,
          nc.sbuf_tensor("expT", [128, 2 * NQ], BF16) as expT,
          nc.sbuf_tensor("invs", [1, NQ], F32) as invs,
          nc.sbuf_tensor("invr", [128, NQ], F32) as invr,
          nc.psum_tensor("ps0", [128, 1024], F32) as ps0,
          nc.psum_tensor("ps1", [128, 1024], F32) as ps1,
          nc.psum_tensor("pv", [128, 1024], F32) as pv,
          nc.psum_tensor("psm", [128, 1024], F32) as psm):
        r_KTb = [Res(), Res()]
        r_Vh = [Res(), Res()]
        r_expT = [Res(), Res()]
        r_ps = [Res(), Res()]
        r_pv, r_psm = Res(), Res()
        r_invs, r_invr = Res(), Res()
        psl = [ps0, ps1]
        ktoks = [None] * HE
        vtoks = [None] * HE
        sctoks = {}
        extoks = {}
        smtoks = {}
        pvtoks = {}

        def emit_load(h):
            sl = h % 2
            kv = KTb[:, sl * S:(sl + 1) * S]
            vv = Vh[:, sl * S:(sl + 1) * S]
            ktoks[h] = sc.step("sync", lambda e, h_=h, k_=kv: e.dma_start(out=k_, in_=kvh[h_, 0, :, :]),
                               deps=r_KTb[sl].wdeps() + [r_kvh[h][0].w], dma=True, stream=f"ktb{sl}")
            r_KTb[sl].wrote(ktoks[h])
            tv = sc.step("sync", lambda e, h_=h, v_=vv: e.dma_start(out=v_, in_=kvh[h_, 1, :, :]),
                         deps=r_Vh[sl].wdeps() + [r_kvh[h][1].w], dma=True, stream=f"vh{sl}")
            r_Vh[sl].wrote(tv)
            vtoks[h] = tv

        def emit_sc_exp(h, kc):
            sl = h % 2
            csl = kc % 2
            ps = psl[csl]
            ev = expT[:, csl * NQ:(csl + 1) * NQ]

            def mk_sc(h_=h, kc_=kc, ps_=ps, sl_=sl):
                def f(te):
                    ins = None
                    for n in range(2):
                        ins = te.matmul(ps_[:, n * 512:(n + 1) * 512],
                                        KTb[:, sl_ * S + kc_ * 128: sl_ * S + kc_ * 128 + 128],
                                        QT[:, h_ * NQ + n * 512: h_ * NQ + n * 512 + 512],
                                        start=True, stop=True)
                    return ins
                return f
            tok_s = sc.step("tensor", mk_sc(), deps=[ktoks[h], r_QT[h].w] + r_ps[csl].wdeps())
            r_KTb[sl].read(tok_s)
            r_QT[h].read(tok_s)
            r_ps[csl].wrote(tok_s)
            tok_e = sc.step("scalar", lambda e, ps_=ps, e_=ev: e.activation(e_, ps_[:], AF.Exp, scale=cscale[:]),
                            deps=[tok_s, t_cs] + r_expT[csl].wdeps())
            r_ps[csl].read(tok_e)
            r_expT[csl].wrote(tok_e)
            sctoks[(h, kc)] = tok_s
            extoks[(h, kc)] = tok_e

        def emit_sum_pv(h, kc):
            sl = h % 2
            csl = kc % 2
            ev = expT[:, csl * NQ:(csl + 1) * NQ]
            vv = Vh[:, sl * S:(sl + 1) * S]
            tok_e = extoks[(h, kc)]

            def mk_sum(ev_=ev, kc_=kc):
                def f(te):
                    ins = None
                    for n in range(2):
                        ins = te.matmul(psm[0:1, n * 512:(n + 1) * 512], ones_b[:],
                                        ev_[:, n * 512:(n + 1) * 512],
                                        start=(kc_ == 0), stop=(kc_ == 15))
                    return ins
                return f
            tok_sm = sc.step("tensor", mk_sum(),
                             deps=[tok_e, t_o1] + (r_psm.wdeps() if kc == 0 else [smtoks[(h, kc - 1)]]))
            r_expT[csl].read(tok_sm)

            def mk_pv(vv_=vv, ev_=ev, kc_=kc):
                def f(te):
                    ins = None
                    for n in range(2):
                        ins = te.matmul(pv[:, n * 512:(n + 1) * 512],
                                        vv_[:, kc_ * 128:(kc_ + 1) * 128],
                                        ev_[:, n * 512:(n + 1) * 512],
                                        start=(kc_ == 0), stop=(kc_ == 15))
                    return ins
                return f
            tok_p = sc.step("tensor", mk_pv(),
                            deps=[tok_e, vtoks[h]] + (r_pv.wdeps() if kc == 0 else [pvtoks[(h, kc - 1)]]))
            r_Vh[sl].read(tok_p)
            r_expT[csl].read(tok_p)
            smtoks[(h, kc)] = tok_sm
            pvtoks[(h, kc)] = tok_p

        def emit_chain(h):
            sl = h % 2
            last_sum = smtoks[(h, 15)]
            last_pv = pvtoks[(h, 15)]
            r_psm.wrote(last_sum)
            r_pv.wrote(last_pv)
            tok_r = sc.step("vector", lambda v: v.reciprocal(invs[:], psm[0:1, :]),
                            deps=[last_sum] + r_invs.wdeps())
            r_psm.read(tok_r)
            r_invs.wrote(tok_r)

            def mk_rep(te):
                ins = None
                for n in range(2):
                    ins = te.matmul(psm[:, n * 512:(n + 1) * 512], ones_r[:],
                                    invs[:, n * 512:(n + 1) * 512], start=True, stop=True)
                return ins
            tok_rp = sc.step("tensor", mk_rep, deps=[tok_r, t_o3] + r_psm.wdeps())
            r_invs.read(tok_rp)
            r_psm.wrote(tok_rp)
            tok_rc = sc.step("vector", lambda v: v.tensor_copy(invr[:], psm[:]),
                             deps=[tok_rp] + r_invr.wdeps())
            r_psm.read(tok_rc)
            r_invr.wrote(tok_rc)
            tok_a = sc.step("vector", lambda v, h_=h: v.tensor_tensor(aT[:, h_ * NQ:(h_ + 1) * NQ], pv[:], invr[:], op=ALU.mult),
                            deps=[last_pv, tok_rc] + r_aT[h].wdeps())
            r_pv.read(tok_a)
            r_invr.read(tok_a)
            r_aT[h].wrote(tok_a)

        emit_load(0)
        emit_load(1)
        emit_sc_exp(0, 0)
        emit_sc_exp(0, 1)
        for h in range(HE):
            for kc in range(16):
                emit_sum_pv(h, kc)
                nk = kc + 2
                if nk <= 15:
                    emit_sc_exp(h, nk)
                elif h + 1 < HE:
                    emit_sc_exp(h + 1, nk - 16)
                if kc % 2 == 1:
                    pump(1)
            if h + 2 < HE:
                emit_load(h + 2)
            emit_chain(h)
        sc.flush()
    while bg_pos[0] < len(bg):
        pump(16)
    sc.flush()
    esQ.close()

    sc.barrier()
    with (nc.sbuf_tensor("woutb", [128, 16 * D], BF16) as woutb,
          nc.sbuf_tensor("boutr", [1, D], F32) as boutr,
          nc.sbuf_tensor("boutrep", [128, D], F32) as boutrep,
          nc.sbuf_tensor("zt", [128, 2 * D], F32) as zt,
          nc.sbuf_tensor("x0t", [128, 2 * D], F32) as x0t,
          nc.sbuf_tensor("x1t", [128, 2 * D], F32) as x1t,
          nc.sbuf_tensor("ln2b", [128, 2 * D], BF16) as ln2b,
          nc.sbuf_tensor("lnz", [128, 2 * D], F32) as lnz,
          nc.sbuf_tensor("l2st", [128, 2048], BF16) as l2st,
          nc.psum_tensor("pza", [128, 1024], F32) as pza,
          nc.psum_tensor("pzb", [128, 1024], F32) as pzb,
          nc.psum_tensor("pT2", [128, 2048], BF16) as pT2):
        r_wout = Res()
        r_zt = [Res(), Res()]
        r_x1t = [Res(), Res()]
        r_x0t = [Res(), Res()]
        r_ln2b = [Res(), Res()]
        r_l2st = Res()
        c_pend = []

        def flush_c():
            for qb_, tok_l2_, csl_ in c_pend:
                toks_tr = []
                for kd in range(16):
                    def mk_tr2(kd_=kd, csl2=csl_):
                        def f(te):
                            return te.transpose(pT2[:, kd_ * 128:(kd_ + 1) * 128],
                                                ln2b[:, csl2 * D + kd_ * 128: csl2 * D + kd_ * 128 + 128], ident[:])
                        return f
                    dd = [tok_l2_, t_id] + (r_pT2.wdeps() if kd == 0 else [])
                    toks_tr.append(sc.step("tensor", mk_tr2(), deps=dd))
                r_ln2b[csl_].read(toks_tr[-1])
                r_pT2.wrote(toks_tr[-1])
                tok_e2 = sc.step("vector", lambda v: v.tensor_copy(l2st[:], pT2[:]),
                                 deps=[toks_tr[-1]] + r_l2st.wdeps())
                r_pT2.read(tok_e2)
                r_l2st.wrote(tok_e2)
                tok_l2w = sc.step("gpsimd", lambda g, qb2=qb_: g.dma_start(
                    out=ln2sp.rearrange("kd p q -> p kd q")[:, :, qb2 * 128:(qb2 + 1) * 128],
                    in_=l2st[:].rearrange("p (kd c) -> p kd c", kd=16)),
                    deps=[tok_e2], dma=True, stream="l2w")
                r_l2st.read(tok_l2w)
                r_ln2sp[qb_].wrote(tok_l2w)
            c_pend.clear()
        r_lnz = [Res(), Res()]
        r_pz = [Res(), Res()]
        r_pT2 = Res()
        r_brep = Res()

        t_bo = sc.step("gpsimd", lambda g: g.dma_start(out=boutr[:], in_=bout_v[:, :]), dma=True, stream="bo")
        wtoks = []
        for g in range(4):
            tok = sc.step("gpsimd", lambda gg, g_=g: gg.dma_start(
                out=woutb[:].rearrange("p (t c) -> p t c", t=16)[:, g_ * 4:(g_ + 1) * 4, :],
                in_=wout_t.rearrange("t p c -> p t c")[:, g_ * 4:(g_ + 1) * 4, :]), dma=True, stream="wout")
            wtoks.append(tok)
        r_wout.wrote(wtoks[-1])

        def mk_br(half):
            def f(te):
                ins = None
                pzh = [pza, pzb][half]
                for n in range(2):
                    ins = te.matmul(pzh[:, n * 512:(n + 1) * 512], ones_r[:],
                                    boutr[:, half * 1024 + n * 512: half * 1024 + n * 512 + 512],
                                    start=True, stop=True)
                return ins
            return f
        tok_br0 = sc.step("tensor", mk_br(0), deps=[t_bo, t_o3] + r_pz[0].wdeps())
        r_pz[0].wrote(tok_br0)
        tok_br1 = sc.step("tensor", mk_br(1), deps=[t_bo, t_o3] + r_pz[1].wdeps())
        r_pz[1].wrote(tok_br1)
        tok_bc0 = sc.step("vector", lambda v: v.tensor_copy(boutrep[:, 0:1024], pza[:]), deps=[tok_br0])
        r_pz[0].read(tok_bc0)
        tok_bc1 = sc.step("vector", lambda v: v.tensor_copy(boutrep[:, 1024:2048], pzb[:]), deps=[tok_br1])
        r_pz[1].read(tok_bc1)
        r_brep.wrote(tok_bc1)

        for qb in range(8):
            sl = qb % 2

            ztv = zt[:, sl * D:(sl + 1) * D]
            x1v = x1t[:, sl * D:(sl + 1) * D]
            co = 5 * sl
            rst = r_statl[sl]
            tok_zes = []
            for half in range(2):
                pzh = [pza, pzb][half]

                def mk_z(qb_=qb, half_=half, p_=None):
                    def f(te):
                        ins = None
                        for kd in range(16):
                            for n in range(2):
                                ins = te.matmul(p_[:, n * 512:(n + 1) * 512],
                                                aT[:, kd * NQ + qb_ * 128: kd * NQ + qb_ * 128 + 128],
                                                woutb[:, kd * D + half_ * 1024 + n * 512:
                                                      kd * D + half_ * 1024 + n * 512 + 512],
                                                start=(kd == 0), stop=(kd == 15))
                        return ins
                    return f
                tok_z = sc.step("tensor", mk_z(p_=pzh),
                                deps=[r_wout.w] + [r_aT[k].w for k in range(HE)] + r_pz[half].wdeps())
                for k in range(HE):
                    r_aT[k].read(tok_z)
                r_wout.read(tok_z)
                r_pz[half].wrote(tok_z)
                tok_ze = sc.step("vector", lambda v, z_=ztv, h2_=half, p_=pzh: v.tensor_tensor(
                    z_[:, h2_ * 1024:(h2_ + 1) * 1024], p_[:], boutrep[:, h2_ * 1024:(h2_ + 1) * 1024], op=ALU.add),
                    deps=[tok_z, r_brep.w] + (r_zt[sl].wdeps() if half == 0 else []))
                r_pz[half].read(tok_ze)
                r_brep.read(tok_ze)
                tok_zes.append(tok_ze)
            r_zt[sl].wrote(tok_zes[-1])
            flush_c()
            tok_s = ln_stats(sc, stat, sqs[:], rst, r_sqs, ztv, tok_zes[-1], D, co=co)
            xv = x0t[:, sl * D:(sl + 1) * D]
            tok_x0 = sc.step("gpsimd", lambda g, qb_=qb, x_=xv: g.dma_start(out=x_, in_=xb[qb_ * 128:(qb_ + 1) * 128, :]),
                             deps=r_x0t[sl].wdeps(), dma=True, stream=f"x0{sl}")
            r_x0t[sl].wrote(tok_x0)
            lz = lnz[:, sl * D:(sl + 1) * D]
            tok_ln = sc.step("scalar", lambda e, z_=ztv, c_=co, l_=lz: e.activation(l_, z_, AF.Identity,
                                                                                    bias=stat[:, c_ + 3:c_ + 4], scale=stat[:, c_ + 2:c_ + 3]),
                             deps=[tok_s] + r_lnz[sl].wdeps())
            r_zt[sl].read(tok_ln)
            rst.read(tok_ln)
            r_lnz[sl].wrote(tok_ln)
            tok_x1 = sc.step("vector", lambda v, x_=xv, o_=x1v, l_=lz: v.tensor_tensor(o_, x_, l_, op=ALU.add),
                             deps=[tok_ln, tok_x0] + r_x1t[sl].wdeps())
            r_x0t[sl].read(tok_x1)
            r_lnz[sl].read(tok_x1)
            r_x1t[sl].wrote(tok_x1)
            tok_sp = sc.step("gpsimd", lambda g, qb_=qb, o_=x1v: g.dma_start(out=x1sp[qb_ * 128:(qb_ + 1) * 128, :], in_=o_),
                             deps=[tok_x1], dma=True, stream=f"x1sp{sl}")
            r_x1t[sl].read(tok_sp)
            r_x1sp[qb].wrote(tok_sp)
            tok_u = ln_stats(sc, stat, sqs[:], r_statl[2], r_sqs, x1v, tok_x1, D, co=10)
            csl = qb % 2
            tok_l2 = sc.step("scalar", lambda e, o_=x1v, c_=csl: e.activation(ln2b[:, c_ * D:(c_ + 1) * D], o_, AF.Identity,
                                                                              bias=stat[:, 13:14], scale=stat[:, 12:13]),
                             deps=[tok_u] + r_ln2b[csl].wdeps())
            r_x1t[sl].read(tok_l2)
            r_statl[2].read(tok_l2)
            r_ln2b[csl].wrote(tok_l2)
            c_pend.append((qb, tok_l2, csl))
        flush_c()
        sc.flush()
    esAT.close()

    sc.barrier()
    esD = ExitStack()
    hid = esD.enter_context(nc.sbuf_tensor("hid", [128, 44 * NQ], BF16))
    r_hid = Res()
    with (nc.sbuf_tensor("ln2Tb", [128, 16 * NQ], BF16) as ln2Tb,
          nc.sbuf_tensor("w1s", [128, 2 * 8192], BF16) as w1s,
          nc.sbuf_tensor("w2s", [128, 2 * 8192], BF16) as w2s,
          nc.sbuf_tensor("sig", [128, NQ], BF16) as sig,
          nc.psum_tensor("px1a", [128, 1024], F32) as px1a,
          nc.psum_tensor("px2a", [128, 1024], F32) as px2a,
          nc.psum_tensor("px1b", [128, 1024], F32) as px1b,
          nc.psum_tensor("px2b", [128, 1024], F32) as px2b):
        r_w1s = [Res(), Res()]
        r_w2s = [Res(), Res()]
        r_sig = Res()
        r_px1 = [Res(), Res()]
        r_px2 = [Res(), Res()]
        px1l, px2l = [px1a, px1b], [px2a, px2b]
        tok_l2rh = []
        for nh in range(2):
            tok_l2rh.append(sc.step("sync", lambda e, nh_=nh: e.dma_start(
                out=ln2Tb[:].rearrange("p (kd q) -> p kd q", kd=16)[:, :, nh_ * 512:(nh_ + 1) * 512],
                in_=ln2sp.rearrange("kd p q -> p kd q")[:, :, nh_ * 512:(nh_ + 1) * 512]),
                deps=[r_ln2sp[i].w for i in range(nh * 4, nh * 4 + 4)], dma=True, stream="l2r"))
        tok_l2r = tok_l2rh[1]

        for grp in range(11):
            gsl = grp % 2
            tok_w1 = None
            tok_w2 = None
            tok_w1 = sc.step("sync", lambda e, g_=grp, s_=gsl: e.dma_start(
                out=w1s[:, s_ * 8192:s_ * 8192 + 8192].rearrange("p (kd h) -> p kd h", kd=16),
                in_=w1q.rearrange("(kd p) h -> p kd h", p=128)[:, :, g_ * 512:(g_ + 1) * 512]),
                deps=r_w1s[gsl].wdeps() + r_w1q.r, dma=True, stream=f"w1s{gsl}")
            tok_w2 = sc.step("scalar", lambda e, g_=grp, s_=gsl: e.dma_start(
                out=w2s[:, s_ * 8192:s_ * 8192 + 8192].rearrange("p (kd h) -> p kd h", kd=16),
                in_=w2q.rearrange("(kd p) h -> p kd h", p=128)[:, :, g_ * 512:(g_ + 1) * 512]),
                deps=r_w2s[gsl].wdeps() + r_w2q.r, dma=True, stream=f"w2s{gsl}")
            r_w1s[gsl].wrote(tok_w1)
            r_w2s[gsl].wrote(tok_w2)
            for sub in range(4):
                hc = grp * 4 + sub
                csl = hc % 2
                p1, p2 = px1l[csl], px2l[csl]

                def mk_x(buf, gsl_=gsl, sub_=sub, p_=None):
                    def f(te):
                        ins = None
                        for kd in range(16):
                            for n in range(2):
                                ins = te.matmul(p_[:, n * 512:(n + 1) * 512],
                                                buf[:, gsl_ * 8192 + kd * 512 + sub_ * 128:
                                                    gsl_ * 8192 + kd * 512 + sub_ * 128 + 128],
                                                ln2Tb[:, kd * NQ + n * 512: kd * NQ + n * 512 + 512],
                                                start=(kd == 0), stop=(kd == 15))
                        return ins
                    return f
                tok_x1m = sc.step("tensor", mk_x(w1s, p_=p1), deps=[tok_w1, tok_l2r] + r_px1[csl].wdeps())
                r_w1s[gsl].read(tok_x1m)
                r_px1[csl].wrote(tok_x1m)
                tok_x2m = sc.step("tensor", mk_x(w2s, p_=p2), deps=[tok_w2, tok_l2r] + r_px2[csl].wdeps())
                r_w2s[gsl].read(tok_x2m)
                r_px2[csl].wrote(tok_x2m)
                tok_sg = sc.step("scalar", lambda e, p_=p1: e.activation(sig[:], p_[:], AF.Sigmoid),
                                 deps=[tok_x1m] + r_sig.wdeps())
                r_px1[csl].read(tok_sg)
                r_sig.wrote(tok_sg)
                tok_h = sc.step("vector", lambda v, hc_=hc, p_=p2: v.tensor_tensor(
                    hid[:, hc_ * NQ:(hc_ + 1) * NQ], p_[:], sig[:], op=ALU.mult),
                    deps=[tok_x2m, tok_sg] + r_hid.rdeps() + ([r_hid.w] if r_hid.w else []))
                r_px2[csl].read(tok_h)
                r_sig.read(tok_h)
                r_hid.wrote(tok_h)
        sc.flush()
    hid_tok = r_hid.w

    sc.barrier()
    with (nc.sbuf_tensor("w3s", [128, 2 * 11264], BF16) as w3s,
          nc.sbuf_tensor("ffnT", [128, 16 * NQ], BF16) as ffnT,
          nc.sbuf_tensor("frm", [128, D], BF16) as frm,
          nc.sbuf_tensor("x1r", [128, 2 * D], F32) as x1r,
          nc.sbuf_tensor("outb", [128, D], F32) as outb,
          nc.psum_tensor("pf0", [128, 1024], F32) as pf0,
          nc.psum_tensor("pf1", [128, 1024], F32) as pf1,
          nc.psum_tensor("pT3", [128, 2048], BF16) as pT3):
        r_w3s = [Res(), Res()]
        r_ffnT = Res()
        r_frm = Res()
        r_x1r = [Res(), Res()]
        r_outb = Res()
        r_pf = [Res(), Res()]
        r_pT3 = Res()
        pfl = [pf0, pf1]
        out_toks = []

        for dg in range(8):
            gsl = dg % 2
            tok_ws = None
            tok_ws = sc.step("sync", lambda e, dg_=dg, s_=gsl: e.dma_start(
                out=w3s[:, s_ * 11264:(s_ + 1) * 11264].rearrange("p (hc d) -> p hc d", hc=44),
                in_=w3q.rearrange("(hc p) d -> p hc d", p=128)[:, :, dg_ * 256:(dg_ + 1) * 256]),
                deps=r_w3s[gsl].wdeps() + r_w3q.r, dma=True, stream=f"w3s{gsl}")
            tok_wsl = [tok_ws]
            r_w3s[gsl].wrote(tok_ws)
            for sub in range(2):
                dc = dg * 2 + sub
                csl = dc % 2
                pf = pfl[csl]

                def mk_f(sub_=sub, pf_=pf, gsl_=gsl):
                    def f(te):
                        ins = None
                        for hc in range(44):
                            for n in range(2):
                                ins = te.matmul(pf_[:, n * 512:(n + 1) * 512],
                                                w3s[:, gsl_ * 11264 + hc * 256 + sub_ * 128:
                                                    gsl_ * 11264 + hc * 256 + sub_ * 128 + 128],
                                                hid[:, hc * NQ + n * 512: hc * NQ + n * 512 + 512],
                                                start=(hc == 0), stop=(hc == 43))
                        return ins
                    return f
                tok_f = sc.step("tensor", mk_f(), deps=[tok_wsl[-1], hid_tok] + r_pf[csl].wdeps())
                r_w3s[gsl].read(tok_f)
                r_pf[csl].wrote(tok_f)
                tok_fe = sc.step("scalar", lambda e, dc_=dc, pf_=pf: e.activation(
                    ffnT[:, dc_ * NQ:(dc_ + 1) * NQ], pf_[:], AF.Identity),
                    deps=[tok_f] + r_ffnT.rdeps() + ([r_ffnT.w] if r_ffnT.w else []))
                r_pf[csl].read(tok_fe)
                r_ffnT.wrote(tok_fe)
        ffnT_tok = r_ffnT.w

        for qb in range(8):
            toks_tr = []
            for dc in range(16):
                def mk_tr3(qb_=qb, dc_=dc):
                    def f(te):
                        return te.transpose(pT3[:, dc_ * 128:(dc_ + 1) * 128],
                                            ffnT[:, dc_ * NQ + qb_ * 128: dc_ * NQ + qb_ * 128 + 128],
                                            ident[:])
                    return f
                dd = [ffnT_tok, t_id] + (r_pT3.wdeps() if dc == 0 else [])
                toks_tr.append(sc.step("tensor", mk_tr3(), deps=dd))
            r_pT3.wrote(toks_tr[-1])
            tok_fr = sc.step("vector", lambda v: v.tensor_copy(frm[:], pT3[:]),
                             deps=[toks_tr[-1]] + r_frm.wdeps())
            r_pT3.read(tok_fr)
            r_frm.wrote(tok_fr)
            sl = qb % 2
            co = 5 * sl
            rst = r_statl[sl]
            tok_s = ln_stats(sc, stat, sqs[:], rst, r_sqs, frm[:], tok_fr, D, co=co)
            xv = x1r[:, sl * D:(sl + 1) * D]
            tok_x1 = sc.step("scalar", lambda e, qb_=qb, x_=xv: e.dma_start(
                out=x_, in_=x1sp[qb_ * 128:(qb_ + 1) * 128, :]),
                deps=r_x1r[sl].wdeps() + [r_x1sp[qb].w], dma=True, stream=f"x1r{sl}")
            r_x1r[sl].wrote(tok_x1)
            tok_ln = sc.step("scalar", lambda e, c_=co: e.activation(sqs[:], frm[:], AF.Identity,
                                                                      bias=stat[:, c_ + 3:c_ + 4], scale=stat[:, c_ + 2:c_ + 3]),
                             deps=[tok_s] + r_sqs.wdeps())
            r_frm.read(tok_ln)
            rst.read(tok_ln)
            r_sqs.wrote(tok_ln)
            tok_o = sc.step("vector", lambda v, x_=xv: v.tensor_tensor(outb[:], x_, sqs[:], op=ALU.add),
                            deps=[tok_ln, tok_x1] + r_outb.wdeps())
            r_x1r[sl].read(tok_o)
            r_sqs.read(tok_o)
            r_outb.wrote(tok_o)
            tok_st = sc.step("gpsimd", lambda g, qb_=qb: g.dma_start(
                out=out[qb_ * 128:(qb_ + 1) * 128, :], in_=outb[:]),
                deps=[tok_o], dma=True, stream="out")
            r_outb.read(tok_st)
            out_toks.append(tok_st)

        sc.step("gpsimd", lambda g: g.engine_nop(), deps=out_toks)
        sc.flush()
    esD.close()
    es.close()
    return nc


_NC_CACHE = {}


def kernel(L0_attn_x, L0_attn_w_qkv, L0_attn_b_qkv, L0_attn_w_out, L0_attn_b_out,
           L0_ffn_w1, L0_ffn_w2, L0_ffn_w3):
    x = np.ascontiguousarray(L0_attn_x, dtype=np.float32)
    wqkv = np.ascontiguousarray(L0_attn_w_qkv, dtype=np.float32)
    bqkv = np.ascontiguousarray(L0_attn_b_qkv, dtype=np.float32)
    wout = np.ascontiguousarray(L0_attn_w_out, dtype=np.float32)
    bout = np.ascontiguousarray(L0_attn_b_out, dtype=np.float32)
    w1 = np.ascontiguousarray(L0_ffn_w1, dtype=np.float32)
    w2 = np.ascontiguousarray(L0_ffn_w2, dtype=np.float32)
    w3 = np.ascontiguousarray(L0_ffn_w3, dtype=np.float32)

    wqkv_t = np.ascontiguousarray(wqkv.reshape(16, 128, 3, 16, 128).transpose(3, 1, 2, 0, 4)
                                  .reshape(16, 128, 6144))
    bqkv_pm = np.ascontiguousarray(bqkv.reshape(48, 128).T)
    wout_t = np.ascontiguousarray(wout.reshape(16, 128, D))
    bout_v = np.ascontiguousarray(bout.reshape(1, D))
    w1_t = np.ascontiguousarray(w1.T).reshape(44, 128, D)
    w2_t = np.ascontiguousarray(w2.T).reshape(44, 128, D)
    w3_t = np.ascontiguousarray(w3.T).reshape(44, 128, D)
    identw = np.eye(128, dtype=ml_dtypes.bfloat16)

    if "nc" not in _NC_CACHE:
        _NC_CACHE["nc"] = build_nc()
    nc = _NC_CACHE["nc"]

    g1 = w1_t.reshape(88, 128, 1024)
    g2 = w2_t.reshape(88, 128, 1024)
    g3 = w3_t.reshape(88, 128, 1024)
    in_maps = []
    for c in range(8):
        b, half = c // 2, c % 2
        if half == 0:
            xbc = x[b]
        else:
            xbc = np.ascontiguousarray(np.concatenate([x[b, NQ:], x[b, :NQ]], axis=0))
        gsrc_c = np.ascontiguousarray(np.stack([g1[c * 11:(c + 1) * 11],
                                                g2[c * 11:(c + 1) * 11],
                                                g3[c * 11:(c + 1) * 11]]))
        in_maps.append({"xb": xbc, "wqkv_t": wqkv_t, "bqkv_pm": bqkv_pm, "wout_t": wout_t,
                        "bout_v": bout_v, "w1_t": w1_t, "w2_t": w2_t, "w3_t": w3_t,
                        "gsrc": gsrc_c, "identw": identw})
    res = run_bass_kernel_spmd(nc, in_maps, core_ids=list(range(8)))
    outp = np.empty((4, S, D), dtype=np.float32)
    for c in range(8):
        b, half = c // 2, c % 2
        outp[b, half * NQ:(half + 1) * NQ] = res.results[c]["out"]
    return outp


if __name__ == "__main__":
    import reference
    inp = reference.setup_inputs()
    inp = {k: np.asarray(v) for k, v in inp.items()}
    o = kernel(**inp)
    e = np.asarray(reference.reference(**inp))
    rv = float(((o - e) ** 2).mean() / (e ** 2).mean())
    print("resid_var_ratio:", rv, "maxabs:", float(np.abs(o - e).max()))


# version 21 (final)
# speedup vs baseline: 1.0084x; 1.0084x over previous
import numpy as np
import ml_dtypes
from contextlib import ExitStack
import concourse.bass as bass
import concourse.mybir as mybir
from concourse.bass_utils import run_bass_kernel_spmd

F32 = mybir.dt.float32
BF16 = mybir.dt.bfloat16
AF = mybir.ActivationFunctionType
ALU = mybir.AluOpType
AX = mybir.AxisListType.X

D = 2048
HE = 16
S = 2048
NQ = 1024
HF = 5632
NW = HF * D
SCALE = float(np.float32(1.0) / np.float32(np.sqrt(128.0)))
ENGINES = ("tensor", "vector", "scalar", "sync", "gpsimd")


class Sched:

    def __init__(self, nc, es):
        self.nc = nc
        self.es = es
        self.sem = {n: es.enter_context(nc.semaphore(f"S_{n}")) for n in ENGINES}
        self.cnt = {n: 0 for n in ENGINES}
        self.steps = {n: [] for n in ENGINES}
        self.waited = {n: {} for n in ENGINES}
        self.pending = {n: [] for n in ENGINES}

    def _sem(self, key):
        if key not in self.sem:
            self.sem[key] = self.es.enter_context(self.nc.semaphore(f"S_{key}"))
            self.cnt[key] = 0
        return self.sem[key]

    def step(self, eng, fn, deps=(), dma=False, stream=None, inc=None):
        key = eng
        if dma:
            assert stream is not None, "dma steps must name a stream"
            key = "D_" + stream
            self._sem(key)
        deps = list(deps) + self.pending[eng]
        self.pending[eng] = []
        waits = {}
        for d in deps:
            if d is None:
                continue
            e, v, isd = d
            if v > self.waited[eng].get(e, 0):
                waits[e] = max(waits.get(e, 0), v)
        for e, v in waits.items():
            self.waited[eng][e] = v
        if inc is None:
            inc = 16 if dma else 1
        self.cnt[key] += inc
        self.steps[eng].append((dict(waits), fn, inc, key))
        return (key, self.cnt[key], dma)

    def barrier(self):
        toks = [(e, v, e.startswith("D_")) for e, v in self.cnt.items() if v > 0]
        for e in ENGINES:
            self.pending[e] = list(toks)

    def flush(self):
        nc, sem, steps = self.nc, self.sem, self.steps
        if not any(steps[n] for n in ENGINES):
            return
        with nc.Block() as blk:
            def mk(name):
                def body(eng):
                    for waits, fn, inc, key in steps[name]:
                        for e, v in waits.items():
                            eng.wait_ge(sem[e], v)
                        fn(eng).then_inc(sem[key], inc)
                return body
            for name, reg in (("tensor", blk.tensor), ("vector", blk.vector),
                              ("scalar", blk.scalar), ("sync", blk.sync),
                              ("gpsimd", blk.gpsimd)):
                if steps[name]:
                    reg(mk(name))
        self.steps = {n: [] for n in ENGINES}


class Res:
    def __init__(self):
        self.w = None
        self.r = []

    def rdeps(self):
        return [self.w] if self.w else []

    def wdeps(self):
        return ([self.w] if self.w else []) + self.r

    def wrote(self, tok):
        self.w = tok
        self.r = []

    def read(self, tok):
        self.r.append(tok)


def ln_stats(sc, stat, sqs_v, r_stat, r_sqs, src_ap, src_tok, sq_fd, co=0):
    c0, c1, c2, c3, c4 = co, co + 1, co + 2, co + 3, co + 4
    t1 = sc.step("vector", lambda v: v.reduce_sum(stat[:, c0:c0 + 1], src_ap, axis=AX),
                 deps=[src_tok] + r_stat.wdeps())
    tq = sc.step("scalar", lambda e: e.activation(sqs_v, src_ap, AF.Square, accum_out=stat[:, c1:c1 + 1]),
                 deps=[src_tok] + r_sqs.wdeps() + r_stat.wdeps())
    r_sqs.wrote(tq)
    t3 = sc.step("vector", lambda v: v.tensor_scalar(stat[:, c0:c0 + 1], stat[:, c0:c0 + 1], 1.0 / sq_fd, None, op0=ALU.mult), deps=[t1])
    t4 = sc.step("vector", lambda v: v.tensor_scalar(stat[:, c1:c1 + 1], stat[:, c1:c1 + 1], 1.0 / sq_fd, 1e-5,
                                                     op0=ALU.mult, op1=ALU.add), deps=[tq])
    t5 = sc.step("vector", lambda v: v.tensor_tensor(stat[:, c4:c4 + 1], stat[:, c0:c0 + 1], stat[:, c0:c0 + 1], op=ALU.mult), deps=[t3])
    t6 = sc.step("vector", lambda v: v.tensor_tensor(stat[:, c1:c1 + 1], stat[:, c1:c1 + 1], stat[:, c4:c4 + 1], op=ALU.subtract), deps=[t4, t5])
    t7 = sc.step("scalar", lambda e: e.activation(stat[:, c2:c2 + 1], stat[:, c1:c1 + 1], AF.Sqrt), deps=[t6])
    t8 = sc.step("vector", lambda v: v.reciprocal(stat[:, c2:c2 + 1], stat[:, c2:c2 + 1]), deps=[t7])
    t9 = sc.step("vector", lambda v: v.tensor_scalar(stat[:, c3:c3 + 1], stat[:, c0:c0 + 1], stat[:, c2:c2 + 1], -1.0,
                                                     op0=ALU.mult, op1=ALU.mult), deps=[t8])
    r_stat.wrote(t9)
    return t9


def build_nc():
    nc = bass.Bass("TRN2", target_bir_lowering=False, enable_partition_id=False)
    xb = nc.dram_tensor("xb", [S, D], F32, kind="ExternalInput").ap()
    wqkv_t = nc.dram_tensor("wqkv_t", [HE, 128, 6144], F32, kind="ExternalInput").ap()
    bqkv_pm = nc.dram_tensor("bqkv_pm", [128, 48], F32, kind="ExternalInput").ap()
    wout_t = nc.dram_tensor("wout_t", [16, 128, D], F32, kind="ExternalInput").ap()
    bout_v = nc.dram_tensor("bout_v", [1, D], F32, kind="ExternalInput").ap()
    w1_t = nc.dram_tensor("w1_t", [44, 128, D], F32, kind="ExternalInput").ap()
    w2_t = nc.dram_tensor("w2_t", [44, 128, D], F32, kind="ExternalInput").ap()
    w3_t = nc.dram_tensor("w3_t", [44, 128, D], F32, kind="ExternalInput").ap()
    gsrc = nc.dram_tensor("gsrc", [3, 11, 128, 1024], F32, kind="ExternalInput").ap()
    identw = nc.dram_tensor("identw", [128, 128], BF16, kind="ExternalInput").ap()
    gpart_in = nc.dram_tensor("gpart_in", [1, 8], F32, kind="Internal").ap()
    gpart_out = nc.dram_tensor("gpart_out", [1, 8], F32, kind="Internal").ap()
    out = nc.dram_tensor("out", [NQ, D], F32, kind="ExternalOutput").ap()
    kvh = nc.dram_tensor("kvh", [HE, 2, 128, S], BF16, kind="Internal").ap()
    ln2sp = nc.dram_tensor("ln2sp", [16, 128, NQ], BF16, kind="Internal").ap()
    x1sp = nc.dram_tensor("x1sp", [NQ, D], F32, kind="Internal").ap()
    w1q = nc.dram_tensor("w1q", [D, HF], BF16, kind="Internal").ap()
    w2q = nc.dram_tensor("w2q", [D, HF], BF16, kind="Internal").ap()
    w3q = nc.dram_tensor("w3q", [HF, D], BF16, kind="Internal").ap()
    wq_flat = {"w1": w1q.rearrange("a b -> (a b)"), "w2": w2q.rearrange("a b -> (a b)"),
               "w3": w3q.rearrange("a b -> (a b)")}

    es = ExitStack()
    sc = Sched(nc, es)
    A = es.enter_context

    bqkv = A(nc.sbuf_tensor("bqkv", [128, 48], F32))
    ident = A(nc.sbuf_tensor("ident", [128, 128], BF16))
    ones_b = A(nc.sbuf_tensor("ones_b", [128, 1], BF16))
    ones_f = A(nc.sbuf_tensor("ones_f", [128, 1], F32))
    ones_r = A(nc.sbuf_tensor("ones_r", [1, 128], F32))
    gacc = A(nc.sbuf_tensor("gacc", [128, 3], F32))
    gtmp = A(nc.sbuf_tensor("gtmp", [128, 1], F32))
    tvals = A(nc.sbuf_tensor("tvals", [1, 3], F32))
    gpre = A(nc.sbuf_tensor("gpre", [1, 8], F32))
    tvals8 = A(nc.sbuf_tensor("tvals8", [1, 8], F32))
    trep = A(nc.sbuf_tensor("trep", [128, 3], F32))
    tneg = A(nc.sbuf_tensor("tneg", [128, 3], F32))
    stat = A(nc.sbuf_tensor("stat", [128, 16], F32))
    sqs = A(nc.sbuf_tensor("sqs", [128, D], F32))
    cscale = A(nc.sbuf_tensor("cscale", [128, 1], F32))
    esAT = ExitStack()
    aT = esAT.enter_context(nc.sbuf_tensor("aT", [128, HE * NQ], BF16))

    r_statl = [Res(), Res(), Res()]
    r_sqs = Res()
    r_gacc, r_tval = Res(), Res()
    r_gtmp = Res()
    r_kvh = [[Res(), Res()] for _ in range(HE)]
    r_aT = [Res() for _ in range(HE)]
    r_ln2sp = [Res() for _ in range(8)]
    r_x1sp = [Res() for _ in range(8)]
    r_w1q, r_w2q, r_w3q = Res(), Res(), Res()

    t_bq = sc.step("gpsimd", lambda g: g.dma_start(out=bqkv[:], in_=bqkv_pm[:, :]), dma=True, stream="bq")
    t_id = sc.step("gpsimd", lambda g: g.dma_start(out=ident[:], in_=identw[:, :]), dma=True, stream="id")
    t_o1 = sc.step("vector", lambda v: v.memset(ones_b[:], 1.0))
    t_o2 = sc.step("vector", lambda v: v.memset(ones_f[:], 1.0))
    t_o3 = sc.step("vector", lambda v: v.memset(ones_r[:], 1.0))
    tg0 = sc.step("vector", lambda v: v.memset(gacc[:], 0.0))
    r_gacc.wrote(tg0)
    t_gp0 = sc.step("vector", lambda v: v.memset(gpre[:], 0.0))
    t_cs = sc.step("vector", lambda v: v.memset(cscale[:], SCALE))

    esQ = ExitStack()
    Q_ = esQ.enter_context
    qin = Q_(nc.sbuf_tensor("qin", [128, 2 * D], F32))
    qa = Q_(nc.sbuf_tensor("qa", [128, D], BF16))
    qb = Q_(nc.sbuf_tensor("qb", [128, D], BF16))
    qw = Q_(nc.sbuf_tensor("qw", [128, 2 * D], BF16))
    QT = Q_(nc.sbuf_tensor("QT", [128, HE * NQ], BF16))
    r_qin = [Res(), Res()]
    r_qa, r_qb = Res(), Res()
    r_qw = [Res(), Res()]
    r_QT = [Res() for _ in range(HE)]

    QT_LIST = [("w1", i) for i in range(44)] + [("w2", i) for i in range(44)] + [("w3", i) for i in range(44)]
    GAMMA_LIST = [(m, i) for m in range(3) for i in range(11)]
    WSRC = {"w1": w1_t, "w2": w2_t, "w3": w3_t}
    WCOL = {"w1": 0, "w2": 1, "w3": 2}
    bg = []
    fin = {}

    def mk_gamma(m, i, slot):
        def go():
            qv = qin[:, slot * D:slot * D + 1024]
            td = sc.step("sync", lambda e, m_=m, i_=i, q_=qv: e.dma_start(out=q_, in_=gsrc[m_, i_, :, :]),
                         deps=r_qin[slot].wdeps(), dma=True, stream=f"qin{slot}")
            r_qin[slot].wrote(td)
            tr = sc.step("scalar", lambda e, q_=qv: e.activation(sqs[:, 0:1024], q_, AF.Abs, accum_out=gtmp[:]),
                         deps=r_qin[slot].rdeps() + r_gtmp.wdeps() + r_sqs.wdeps())
            r_sqs.wrote(tr)
            r_qin[slot].read(tr)
            r_gtmp.wrote(tr)
            ta = sc.step("vector", lambda v, c_=m: v.tensor_tensor(gacc[:, c_:c_ + 1], gacc[:, c_:c_ + 1],
                                                                   gtmp[:], op=ALU.add),
                         deps=[tr] + r_gacc.wdeps())
            r_gtmp.read(ta)
            r_gacc.wrote(ta)
        return go

    for j, (m, i) in enumerate(GAMMA_LIST):
        bg.append(mk_gamma(m, i, j % 2))

    def gamma_finalize():
        ps, r_ps = fin["psum"], fin["res"]
        tm = sc.step("tensor", lambda t: t.matmul(ps[0:1, 0:3], ones_f[:], gacc[:, 0:3], start=True, stop=True),
                     deps=[t_o2] + r_gacc.wdeps() + r_ps.wdeps())
        r_gacc.read(tm)
        r_ps.wrote(tm)
        tc0 = sc.step("vector", lambda v: v.tensor_copy(gpre[:, 0:3], ps[0:1, 0:3]), deps=[tm, t_gp0])
        r_ps.read(tc0)
        tdo = sc.step("gpsimd", lambda g: g.dma_start(out=gpart_in, in_=gpre[:]),
                      deps=[tc0], dma=True, stream="gpo")
        tcc = sc.step("gpsimd", lambda g: g.collective_compute(
            "AllReduce", ALU.add, replica_groups=[list(range(8))],
            ins=[gpart_in], outs=[gpart_out]),
            deps=[tdo], dma=True, stream="cc", inc=1)
        tdi = sc.step("sync", lambda e: e.dma_start(out=tvals8[:], in_=gpart_out),
                      deps=[tcc], dma=True, stream="gpi")
        tc = sc.step("vector", lambda v: v.tensor_scalar(tvals[:, :], tvals8[:, 0:3], 0.5 / NW, 0.5e-8,
                                                         op0=ALU.mult, op1=ALU.add), deps=[tdi])
        tr = sc.step("tensor", lambda t: t.matmul(ps[:, 0:3], ones_r[:], tvals[:, :], start=True, stop=True),
                     deps=[tc, t_o3] + r_ps.wdeps())
        r_ps.wrote(tr)
        tt = sc.step("vector", lambda v: v.tensor_copy(trep[:], ps[:, 0:3]), deps=[tr])
        r_ps.read(tt)
        tn = sc.step("vector", lambda v: v.tensor_scalar(tneg[:], trep[:], -1.0, None, op0=ALU.mult), deps=[tt])
        r_tval.wrote(tn)
    bg.append(gamma_finalize)
    IDX_FIN = len(GAMMA_LIST) + 1

    def mk_quant(m, i, slot):
        def go():
            src = WSRC[m]
            c = WCOL[m]
            qv = qin[:, slot * D:(slot + 1) * D]
            qwv = qw[:, slot * D:(slot + 1) * D]
            td = sc.step("sync", lambda e, s_=src, i_=i, q_=qv: e.dma_start(out=q_, in_=s_[i_, :, :]),
                         deps=r_qin[slot].wdeps(), dma=True, stream=f"qin{slot}")
            r_qin[slot].wrote(td)
            ta = sc.step("vector", lambda v, q_=qv, c_=c: v.tensor_scalar(qa[:], q_, trep[:, c_:c_ + 1], None, op0=ALU.is_gt),
                         deps=r_qin[slot].rdeps() + [r_tval.w] + r_qa.wdeps())
            r_qin[slot].read(ta)
            r_qa.wrote(ta)
            tb = sc.step("vector", lambda v, q_=qv, c_=c: v.tensor_scalar(qb[:], q_, tneg[:, c_:c_ + 1], None, op0=ALU.is_lt),
                         deps=r_qin[slot].rdeps() + [r_tval.w] + r_qb.wdeps())
            r_qin[slot].read(tb)
            r_qb.wrote(tb)
            tw = sc.step("vector", lambda v, qw_=qwv: v.tensor_tensor(qw_, qa[:], qb[:], op=ALU.subtract),
                         deps=[r_qa.w, r_qb.w] + r_qw[slot].wdeps())
            r_qa.read(tw)
            r_qb.read(tw)
            r_qw[slot].wrote(tw)
            dap = wq_flat[m][i * 128 * D:(i + 1) * 128 * D].rearrange("(p c) -> p c", p=128)
            grp = {"w1": r_w1q, "w2": r_w2q, "w3": r_w3q}[m]
            ts_ = sc.step("gpsimd", lambda g, qw_=qwv, d_=dap: g.dma_start(out=d_, in_=qw_),
                          deps=r_qw[slot].rdeps(), dma=True, stream=f"qw{slot}")
            r_qw[slot].read(ts_)
            grp.read(ts_)
        return go

    for j, (m, i) in enumerate(QT_LIST):
        bg.append(mk_quant(m, i, j % 2))

    bg_pos = [0]

    def pump(n, cap=None):
        lim = len(bg) if cap is None else min(cap, len(bg))
        while n > 0 and bg_pos[0] < lim:
            bg[bg_pos[0]]()
            bg_pos[0] += 1
            n -= 1

    esA = ExitStack()
    ln1T = esA.enter_context(nc.sbuf_tensor("ln1T", [128, 16 * S], BF16))
    pk0 = esA.enter_context(nc.psum_tensor("pk0", [128, 2048], F32))
    pk1 = esA.enter_context(nc.psum_tensor("pk1", [128, 1024], F32))
    r_ln1T = Res()
    r_pk0a, r_pk0b, r_pk1, r_pk2 = Res(), Res(), Res(), Res()
    fin["psum"], fin["res"] = pk1, r_pk1

    with (nc.sbuf_tensor("xst", [128, 2 * D], F32) as xst,
          nc.sbuf_tensor("ln1b", [128, 2 * D], BF16) as ln1b,
          nc.psum_tensor("pT", [128, 2048], BF16) as pT):
        r_xst = [Res(), Res()]
        r_ln1b = [Res(), Res()]
        r_pT = Res()
        for t in range(16):
            sl = t % 2
            xv = xst[:, sl * D:(sl + 1) * D]
            lv = ln1b[:, sl * D:(sl + 1) * D]
            tok_x = sc.step("sync", lambda e, t_=t, x_=xv: e.dma_start(out=x_, in_=xb[t_ * 128:(t_ + 1) * 128, :]),
                            deps=r_xst[sl].wdeps(), dma=True, stream=f"xin{sl}")
            r_xst[sl].wrote(tok_x)
            co = 5 * (t % 2)
            rst = r_statl[t % 2]
            tok_s = ln_stats(sc, stat, sqs[:], rst, r_sqs, xv, tok_x, D, co=co)
            tok_n = sc.step("scalar", lambda e, x_=xv, l_=lv, c_=co: e.activation(l_, x_, AF.Identity,
                                                                                  bias=stat[:, c_ + 3:c_ + 4], scale=stat[:, c_ + 2:c_ + 3]),
                            deps=[tok_s] + r_ln1b[sl].wdeps())
            r_xst[sl].read(tok_n)
            rst.read(tok_n)
            r_ln1b[sl].wrote(tok_n)
            toks_tr = []
            for kd in range(16):
                def mk_tr(lv_=lv, kd_=kd):
                    def f(te):
                        return te.transpose(pT[:, kd_ * 128:(kd_ + 1) * 128], lv_[:, kd_ * 128:(kd_ + 1) * 128], ident[:])
                    return f
                dd = [tok_n, t_id] + (r_pT.wdeps() if kd == 0 else [])
                toks_tr.append(sc.step("tensor", mk_tr(), deps=dd))
            r_ln1b[sl].read(toks_tr[-1])
            r_pT.wrote(toks_tr[-1])
            tok_e = sc.step("vector", lambda v, t_=t: v.tensor_copy(
                ln1T[:].rearrange("p (kd s) -> p kd s", kd=16)[:, :, t_ * 128:(t_ + 1) * 128],
                pT[:].rearrange("p (kd c) -> p kd c", kd=16)),
                deps=[toks_tr[-1]] + r_ln1T.rdeps() + ([r_ln1T.w] if r_ln1T.w else []))
            r_pT.read(tok_e)
            r_ln1T.wrote(tok_e)
            pump(5, cap=IDX_FIN)
        sc.flush()
    ln1T_tok = r_ln1T.w

    with nc.sbuf_tensor("wqh", [128, 2 * 6144], BF16) as wqh, \
         nc.sbuf_tensor("ktv", [128, 2 * S], BF16) as ktv, \
         nc.sbuf_tensor("vts", [128, 2 * 1024], BF16) as vts, \
         nc.psum_tensor("pvT", [128, 1024], BF16) as pvT:
        r_wqh = [Res(), Res()]
        r_ktv = [Res(), Res()]
        r_vts = [Res(), Res()]
        r_pvT = Res()
        wq_toks = [None] * HE
        vt_pend = []

        def flush_vt():
            for kvv_, half_, tok_ke_, sl_, h_ in vt_pend:
                toks_vt = []
                for c in range(8):
                    def mk_vt(kvv2=kvv_, half2=half_, c2=c):
                        def f(te):
                            return te.transpose(pvT[:, c2 * 128:(c2 + 1) * 128],
                                                kvv2[:, half2 * 1024 + c2 * 128: half2 * 1024 + c2 * 128 + 128],
                                                ident[:])
                        return f
                    dd = [tok_ke_, t_id] + (r_pvT.wdeps() if c == 0 else [])
                    toks_vt.append(sc.step("tensor", mk_vt(), deps=dd))
                r_ktv[sl_].read(toks_vt[-1])
                r_pvT.wrote(toks_vt[-1])
                tok_vc = sc.step("vector", lambda v, half2=half_: v.tensor_copy(
                    vts[:, half2 * 1024:(half2 + 1) * 1024], pvT[:]),
                    deps=[toks_vt[-1]] + r_vts[half_].wdeps())
                r_pvT.read(tok_vc)
                r_vts[half_].wrote(tok_vc)
                tok_vs = sc.step("gpsimd", lambda g, h2=h_, half2=half_: g.dma_start(
                    out=kvh[h2, 1, :, half2 * 1024:(half2 + 1) * 1024],
                    in_=vts[:, half2 * 1024:(half2 + 1) * 1024]),
                    deps=[tok_vc], dma=True, stream=f"ktv{sl_}")
                r_vts[half_].read(tok_vs)
                r_kvh[h_][1].wrote(tok_vs)
            vt_pend.clear()

        def emit_wqh(h):
            sl2 = h % 2
            wv2 = wqh[:, sl2 * 6144:(sl2 + 1) * 6144]
            wq_toks[h] = sc.step("gpsimd", lambda g, h_=h, w_=wv2: g.dma_start(out=w_, in_=wqkv_t[h_, :, :]),
                                 deps=r_wqh[sl2].wdeps(), dma=True, stream=f"wqh{sl2}")
            r_wqh[sl2].wrote(wq_toks[h])

        emit_wqh(0)
        for h in range(HE):
            sl = h % 2
            wv = wqh[:, sl * 6144:(sl + 1) * 6144]
            if h + 1 < HE:
                emit_wqh(h + 1)
            tok_w = wq_toks[h]

            pq = pk1
            r_pq = r_pk1

            def mk_q(wv_=wv, pq_=pq):
                def f(te):
                    ins = None
                    for kd in range(16):
                        for n in range(2):
                            ins = te.matmul(pq_[:, n * 512:(n + 1) * 512],
                                            wv_[:, kd * 128:kd * 128 + 128],
                                            ln1T[:, kd * S + n * 512: kd * S + n * 512 + 512],
                                            start=(kd == 0), stop=(kd == 15))
                    return ins
                return f
            tok_q = sc.step("tensor", mk_q(), deps=[tok_w, ln1T_tok] + r_pq.wdeps())
            r_wqh[sl].read(tok_q)
            r_pq.wrote(tok_q)
            tok_qe = sc.step("scalar", lambda e, h_=h, pq_=pq: e.activation(QT[:, h_ * NQ:(h_ + 1) * NQ], pq_[:, 0:1024],
                                                                            AF.Identity, bias=bqkv[:, h_:h_ + 1], scale=1.0),
                             deps=[tok_q, t_bq] + r_QT[h].wdeps())
            r_pq.read(tok_qe)
            r_QT[h].wrote(tok_qe)
            flush_vt()

            for j, mcol in ((0, 1), (1, 2)):
                kvv = ktv[:, sl * S:(sl + 1) * S]
                bcol = 16 * mcol + h
                tok_kes = []
                for half in range(2):
                    phalf = [r_pk0a, r_pk0b][half]
                    pap = pk0[:, half * 1024:(half + 1) * 1024]
                    def mk_kv(wv_=wv, mcol_=mcol, half_=half, pap_=pap):
                        def f(te):
                            ins = None
                            for kd in range(16):
                                for n in range(2):
                                    ins = te.matmul(pap_[:, n * 512:(n + 1) * 512],
                                                    wv_[:, mcol_ * 2048 + kd * 128: mcol_ * 2048 + kd * 128 + 128],
                                                    ln1T[:, kd * S + half_ * 1024 + n * 512:
                                                         kd * S + half_ * 1024 + n * 512 + 512],
                                                    start=(kd == 0), stop=(kd == 15))
                            return ins
                        return f
                    tok_kv = sc.step("tensor", mk_kv(), deps=[tok_w, ln1T_tok] + phalf.wdeps())
                    r_wqh[sl].read(tok_kv)
                    phalf.wrote(tok_kv)
                    tok_ke = sc.step("scalar", lambda e, b_=bcol, k_=kvv, h2_=half, p_=pap: e.activation(
                        k_[:, h2_ * 1024:(h2_ + 1) * 1024], p_, AF.Identity, bias=bqkv[:, b_:b_ + 1], scale=1.0),
                        deps=[tok_kv, t_bq] + (r_ktv[sl].wdeps() if half == 0 else []))
                    phalf.read(tok_ke)
                    tok_kes.append(tok_ke)
                    if j == 1:
                        vt_pend.append((kvv, half, tok_ke, sl, h))
                if j == 0:
                    r_ktv[sl].wrote(tok_kes[-1])
                    tok_ks = sc.step("gpsimd", lambda g, h_=h, k_=kvv: g.dma_start(out=kvh[h_, 0, :, :], in_=k_),
                                     deps=tok_kes, dma=True, stream=f"ktv{sl}")
                    r_ktv[sl].read(tok_ks)
                    r_kvh[h][0].wrote(tok_ks)
                else:
                    r_ktv[sl].wrote(tok_kes[-1])
            pump(5)
        flush_vt()
        sc.flush()
    esA.close()

    sc.barrier()
    with (nc.sbuf_tensor("KTb", [128, 2 * S], BF16) as KTb,
          nc.sbuf_tensor("Vh", [128, 2 * S], BF16) as Vh,
          nc.sbuf_tensor("expT", [128, 2 * NQ], BF16) as expT,
          nc.sbuf_tensor("invs", [1, NQ], F32) as invs,
          nc.sbuf_tensor("invr", [128, NQ], F32) as invr,
          nc.psum_tensor("ps0", [128, 1024], F32) as ps0,
          nc.psum_tensor("ps1", [128, 1024], F32) as ps1,
          nc.psum_tensor("pv", [128, 1024], F32) as pv,
          nc.psum_tensor("psm", [128, 1024], F32) as psm):
        r_KTb = [Res(), Res()]
        r_Vh = [Res(), Res()]
        r_expT = [Res(), Res()]
        r_ps = [Res(), Res()]
        r_pv, r_psm = Res(), Res()
        r_invs, r_invr = Res(), Res()
        psl = [ps0, ps1]
        ktoks = [None] * HE
        vtoks = [None] * HE
        sctoks = {}
        extoks = {}
        smtoks = {}
        pvtoks = {}

        def emit_load(h):
            sl = h % 2
            kv = KTb[:, sl * S:(sl + 1) * S]
            vv = Vh[:, sl * S:(sl + 1) * S]
            ktoks[h] = sc.step("sync", lambda e, h_=h, k_=kv: e.dma_start(out=k_, in_=kvh[h_, 0, :, :]),
                               deps=r_KTb[sl].wdeps() + [r_kvh[h][0].w], dma=True, stream=f"ktb{sl}")
            r_KTb[sl].wrote(ktoks[h])
            tv = sc.step("sync", lambda e, h_=h, v_=vv: e.dma_start(out=v_, in_=kvh[h_, 1, :, :]),
                         deps=r_Vh[sl].wdeps() + [r_kvh[h][1].w], dma=True, stream=f"vh{sl}")
            r_Vh[sl].wrote(tv)
            vtoks[h] = tv

        def emit_sc_exp(h, kc):
            sl = h % 2
            csl = kc % 2
            ps = psl[csl]
            ev = expT[:, csl * NQ:(csl + 1) * NQ]

            def mk_sc(h_=h, kc_=kc, ps_=ps, sl_=sl):
                def f(te):
                    ins = None
                    for n in range(2):
                        ins = te.matmul(ps_[:, n * 512:(n + 1) * 512],
                                        KTb[:, sl_ * S + kc_ * 128: sl_ * S + kc_ * 128 + 128],
                                        QT[:, h_ * NQ + n * 512: h_ * NQ + n * 512 + 512],
                                        start=True, stop=True)
                    return ins
                return f
            tok_s = sc.step("tensor", mk_sc(), deps=[ktoks[h], r_QT[h].w] + r_ps[csl].wdeps())
            r_KTb[sl].read(tok_s)
            r_QT[h].read(tok_s)
            r_ps[csl].wrote(tok_s)
            tok_e = sc.step("scalar", lambda e, ps_=ps, e_=ev: e.activation(e_, ps_[:], AF.Exp, scale=cscale[:]),
                            deps=[tok_s, t_cs] + r_expT[csl].wdeps())
            r_ps[csl].read(tok_e)
            r_expT[csl].wrote(tok_e)
            sctoks[(h, kc)] = tok_s
            extoks[(h, kc)] = tok_e

        def emit_sum_pv(h, kc):
            sl = h % 2
            csl = kc % 2
            ev = expT[:, csl * NQ:(csl + 1) * NQ]
            vv = Vh[:, sl * S:(sl + 1) * S]
            tok_e = extoks[(h, kc)]

            def mk_sum(ev_=ev, kc_=kc):
                def f(te):
                    ins = None
                    for n in range(2):
                        ins = te.matmul(psm[0:1, n * 512:(n + 1) * 512], ones_b[:],
                                        ev_[:, n * 512:(n + 1) * 512],
                                        start=(kc_ == 0), stop=(kc_ == 15))
                    return ins
                return f
            tok_sm = sc.step("tensor", mk_sum(),
                             deps=[tok_e, t_o1] + (r_psm.wdeps() if kc == 0 else [smtoks[(h, kc - 1)]]))
            r_expT[csl].read(tok_sm)

            def mk_pv(vv_=vv, ev_=ev, kc_=kc):
                def f(te):
                    ins = None
                    for n in range(2):
                        ins = te.matmul(pv[:, n * 512:(n + 1) * 512],
                                        vv_[:, kc_ * 128:(kc_ + 1) * 128],
                                        ev_[:, n * 512:(n + 1) * 512],
                                        start=(kc_ == 0), stop=(kc_ == 15))
                    return ins
                return f
            tok_p = sc.step("tensor", mk_pv(),
                            deps=[tok_e, vtoks[h]] + (r_pv.wdeps() if kc == 0 else [pvtoks[(h, kc - 1)]]))
            r_Vh[sl].read(tok_p)
            r_expT[csl].read(tok_p)
            smtoks[(h, kc)] = tok_sm
            pvtoks[(h, kc)] = tok_p

        def emit_chain(h):
            sl = h % 2
            last_sum = smtoks[(h, 15)]
            last_pv = pvtoks[(h, 15)]
            r_psm.wrote(last_sum)
            r_pv.wrote(last_pv)
            tok_r = sc.step("scalar", lambda e: e.activation(invs[:], psm[0:1, :], AF.Identity),
                            deps=[last_sum] + r_invs.wdeps())
            r_psm.read(tok_r)
            r_invs.wrote(tok_r)

            def mk_rep(te):
                ins = None
                for n in range(2):
                    ins = te.matmul(psm[:, n * 512:(n + 1) * 512], ones_r[:],
                                    invs[:, n * 512:(n + 1) * 512], start=True, stop=True)
                return ins
            tok_rp = sc.step("tensor", mk_rep, deps=[tok_r, t_o3] + r_psm.wdeps())
            r_invs.read(tok_rp)
            r_psm.wrote(tok_rp)
            tok_rc = sc.step("scalar", lambda e: e.activation(invr[:], psm[:], AF.Identity),
                             deps=[tok_rp] + r_invr.wdeps())
            r_psm.read(tok_rc)
            r_invr.wrote(tok_rc)
            tok_ri = sc.step("vector", lambda v: v.reciprocal(invr[:], invr[:]), deps=[tok_rc])
            r_invr.wrote(tok_ri)
            tok_a = sc.step("vector", lambda v, h_=h: v.tensor_tensor(aT[:, h_ * NQ:(h_ + 1) * NQ], pv[:], invr[:], op=ALU.mult),
                            deps=[last_pv, tok_ri] + r_aT[h].wdeps())
            r_pv.read(tok_a)
            r_invr.read(tok_a)
            r_aT[h].wrote(tok_a)

        emit_load(0)
        emit_load(1)
        emit_sc_exp(0, 0)
        emit_sc_exp(0, 1)
        for h in range(HE):
            for kc in range(16):
                emit_sum_pv(h, kc)
                nk = kc + 2
                if nk <= 15:
                    emit_sc_exp(h, nk)
                elif h + 1 < HE:
                    emit_sc_exp(h + 1, nk - 16)
            if h + 2 < HE:
                emit_load(h + 2)
            emit_chain(h)
            pump(8)
        sc.flush()
    while bg_pos[0] < len(bg):
        pump(16)
    sc.flush()
    esQ.close()

    sc.barrier()
    with (nc.sbuf_tensor("woutb", [128, 16 * D], BF16) as woutb,
          nc.sbuf_tensor("boutr", [1, D], F32) as boutr,
          nc.sbuf_tensor("boutrep", [128, D], F32) as boutrep,
          nc.sbuf_tensor("zt", [128, 2 * D], F32) as zt,
          nc.sbuf_tensor("x0t", [128, 2 * D], F32) as x0t,
          nc.sbuf_tensor("x1t", [128, 2 * D], F32) as x1t,
          nc.sbuf_tensor("ln2b", [128, 2 * D], BF16) as ln2b,
          nc.sbuf_tensor("lnz", [128, 2 * D], F32) as lnz,
          nc.sbuf_tensor("l2st", [128, 2048], BF16) as l2st,
          nc.psum_tensor("pza", [128, 1024], F32) as pza,
          nc.psum_tensor("pzb", [128, 1024], F32) as pzb,
          nc.psum_tensor("pT2", [128, 2048], BF16) as pT2):
        r_wout = Res()
        r_zt = [Res(), Res()]
        r_x1t = [Res(), Res()]
        r_x0t = [Res(), Res()]
        r_ln2b = [Res(), Res()]
        r_l2st = Res()
        c_pend = []

        def flush_c():
            for qb_, tok_l2_, csl_ in c_pend:
                toks_tr = []
                for kd in range(16):
                    def mk_tr2(kd_=kd, csl2=csl_):
                        def f(te):
                            return te.transpose(pT2[:, kd_ * 128:(kd_ + 1) * 128],
                                                ln2b[:, csl2 * D + kd_ * 128: csl2 * D + kd_ * 128 + 128], ident[:])
                        return f
                    dd = [tok_l2_, t_id] + (r_pT2.wdeps() if kd == 0 else [])
                    toks_tr.append(sc.step("tensor", mk_tr2(), deps=dd))
                r_ln2b[csl_].read(toks_tr[-1])
                r_pT2.wrote(toks_tr[-1])
                tok_e2 = sc.step("vector", lambda v: v.tensor_copy(l2st[:], pT2[:]),
                                 deps=[toks_tr[-1]] + r_l2st.wdeps())
                r_pT2.read(tok_e2)
                r_l2st.wrote(tok_e2)
                tok_l2w = sc.step("gpsimd", lambda g, qb2=qb_: g.dma_start(
                    out=ln2sp.rearrange("kd p q -> p kd q")[:, :, qb2 * 128:(qb2 + 1) * 128],
                    in_=l2st[:].rearrange("p (kd c) -> p kd c", kd=16)),
                    deps=[tok_e2], dma=True, stream="l2w")
                r_l2st.read(tok_l2w)
                r_ln2sp[qb_].wrote(tok_l2w)
            c_pend.clear()
        r_lnz = [Res(), Res()]
        r_pz = [Res(), Res()]
        r_pT2 = Res()
        r_brep = Res()

        t_bo = sc.step("gpsimd", lambda g: g.dma_start(out=boutr[:], in_=bout_v[:, :]), dma=True, stream="bo")
        wtoks = []
        for g in range(4):
            tok = sc.step("gpsimd", lambda gg, g_=g: gg.dma_start(
                out=woutb[:].rearrange("p (t c) -> p t c", t=16)[:, g_ * 4:(g_ + 1) * 4, :],
                in_=wout_t.rearrange("t p c -> p t c")[:, g_ * 4:(g_ + 1) * 4, :]), dma=True, stream="wout")
            wtoks.append(tok)
        r_wout.wrote(wtoks[-1])

        def mk_br(half):
            def f(te):
                ins = None
                pzh = [pza, pzb][half]
                for n in range(2):
                    ins = te.matmul(pzh[:, n * 512:(n + 1) * 512], ones_r[:],
                                    boutr[:, half * 1024 + n * 512: half * 1024 + n * 512 + 512],
                                    start=True, stop=True)
                return ins
            return f
        tok_br0 = sc.step("tensor", mk_br(0), deps=[t_bo, t_o3] + r_pz[0].wdeps())
        r_pz[0].wrote(tok_br0)
        tok_br1 = sc.step("tensor", mk_br(1), deps=[t_bo, t_o3] + r_pz[1].wdeps())
        r_pz[1].wrote(tok_br1)
        tok_bc0 = sc.step("vector", lambda v: v.tensor_copy(boutrep[:, 0:1024], pza[:]), deps=[tok_br0])
        r_pz[0].read(tok_bc0)
        tok_bc1 = sc.step("vector", lambda v: v.tensor_copy(boutrep[:, 1024:2048], pzb[:]), deps=[tok_br1])
        r_pz[1].read(tok_bc1)
        r_brep.wrote(tok_bc1)

        for qb in range(8):
            sl = qb % 2

            ztv = zt[:, sl * D:(sl + 1) * D]
            x1v = x1t[:, sl * D:(sl + 1) * D]
            co = 5 * sl
            rst = r_statl[sl]
            tok_zes = []
            for half in range(2):
                pzh = [pza, pzb][half]

                def mk_z(qb_=qb, half_=half, p_=None):
                    def f(te):
                        ins = None
                        for kd in range(16):
                            for n in range(2):
                                ins = te.matmul(p_[:, n * 512:(n + 1) * 512],
                                                aT[:, kd * NQ + qb_ * 128: kd * NQ + qb_ * 128 + 128],
                                                woutb[:, kd * D + half_ * 1024 + n * 512:
                                                      kd * D + half_ * 1024 + n * 512 + 512],
                                                start=(kd == 0), stop=(kd == 15))
                        return ins
                    return f
                tok_z = sc.step("tensor", mk_z(p_=pzh),
                                deps=[r_wout.w] + [r_aT[k].w for k in range(HE)] + r_pz[half].wdeps())
                for k in range(HE):
                    r_aT[k].read(tok_z)
                r_wout.read(tok_z)
                r_pz[half].wrote(tok_z)
                tok_ze = sc.step("vector", lambda v, z_=ztv, h2_=half, p_=pzh: v.tensor_tensor(
                    z_[:, h2_ * 1024:(h2_ + 1) * 1024], p_[:], boutrep[:, h2_ * 1024:(h2_ + 1) * 1024], op=ALU.add),
                    deps=[tok_z, r_brep.w] + (r_zt[sl].wdeps() if half == 0 else []))
                r_pz[half].read(tok_ze)
                r_brep.read(tok_ze)
                tok_zes.append(tok_ze)
            r_zt[sl].wrote(tok_zes[-1])
            flush_c()
            tok_s = ln_stats(sc, stat, sqs[:], rst, r_sqs, ztv, tok_zes[-1], D, co=co)
            xv = x0t[:, sl * D:(sl + 1) * D]
            tok_x0 = sc.step("gpsimd", lambda g, qb_=qb, x_=xv: g.dma_start(out=x_, in_=xb[qb_ * 128:(qb_ + 1) * 128, :]),
                             deps=r_x0t[sl].wdeps(), dma=True, stream=f"x0{sl}")
            r_x0t[sl].wrote(tok_x0)
            lz = lnz[:, sl * D:(sl + 1) * D]
            tok_ln = sc.step("scalar", lambda e, z_=ztv, c_=co, l_=lz: e.activation(l_, z_, AF.Identity,
                                                                                    bias=stat[:, c_ + 3:c_ + 4], scale=stat[:, c_ + 2:c_ + 3]),
                             deps=[tok_s] + r_lnz[sl].wdeps())
            r_zt[sl].read(tok_ln)
            rst.read(tok_ln)
            r_lnz[sl].wrote(tok_ln)
            tok_x1 = sc.step("vector", lambda v, x_=xv, o_=x1v, l_=lz: v.tensor_tensor(o_, x_, l_, op=ALU.add),
                             deps=[tok_ln, tok_x0] + r_x1t[sl].wdeps())
            r_x0t[sl].read(tok_x1)
            r_lnz[sl].read(tok_x1)
            r_x1t[sl].wrote(tok_x1)
            tok_sp = sc.step("gpsimd", lambda g, qb_=qb, o_=x1v: g.dma_start(out=x1sp[qb_ * 128:(qb_ + 1) * 128, :], in_=o_),
                             deps=[tok_x1], dma=True, stream=f"x1sp{sl}")
            r_x1t[sl].read(tok_sp)
            r_x1sp[qb].wrote(tok_sp)
            tok_u = ln_stats(sc, stat, sqs[:], r_statl[2], r_sqs, x1v, tok_x1, D, co=10)
            csl = qb % 2
            tok_l2 = sc.step("scalar", lambda e, o_=x1v, c_=csl: e.activation(ln2b[:, c_ * D:(c_ + 1) * D], o_, AF.Identity,
                                                                              bias=stat[:, 13:14], scale=stat[:, 12:13]),
                             deps=[tok_u] + r_ln2b[csl].wdeps())
            r_x1t[sl].read(tok_l2)
            r_statl[2].read(tok_l2)
            r_ln2b[csl].wrote(tok_l2)
            c_pend.append((qb, tok_l2, csl))
        flush_c()
        sc.flush()
    esAT.close()

    sc.barrier()
    esD = ExitStack()
    hid = esD.enter_context(nc.sbuf_tensor("hid", [128, 44 * NQ], BF16))
    r_hid = Res()
    with (nc.sbuf_tensor("ln2Tb", [128, 16 * NQ], BF16) as ln2Tb,
          nc.sbuf_tensor("w1s", [128, 2 * 8192], BF16) as w1s,
          nc.sbuf_tensor("w2s", [128, 2 * 8192], BF16) as w2s,
          nc.sbuf_tensor("sig", [128, NQ], BF16) as sig,
          nc.psum_tensor("px1a", [128, 1024], F32) as px1a,
          nc.psum_tensor("px2a", [128, 1024], F32) as px2a,
          nc.psum_tensor("px1b", [128, 1024], F32) as px1b,
          nc.psum_tensor("px2b", [128, 1024], F32) as px2b):
        r_w1s = [Res(), Res()]
        r_w2s = [Res(), Res()]
        r_sig = Res()
        r_px1 = [Res(), Res()]
        r_px2 = [Res(), Res()]
        px1l, px2l = [px1a, px1b], [px2a, px2b]
        tok_l2rh = []
        for nh in range(2):
            tok_l2rh.append(sc.step("sync", lambda e, nh_=nh: e.dma_start(
                out=ln2Tb[:].rearrange("p (kd q) -> p kd q", kd=16)[:, :, nh_ * 512:(nh_ + 1) * 512],
                in_=ln2sp.rearrange("kd p q -> p kd q")[:, :, nh_ * 512:(nh_ + 1) * 512]),
                deps=[r_ln2sp[i].w for i in range(nh * 4, nh * 4 + 4)], dma=True, stream="l2r"))
        tok_l2r = tok_l2rh[1]

        for grp in range(11):
            gsl = grp % 2
            tok_w1 = None
            tok_w2 = None
            tok_w1 = sc.step("sync", lambda e, g_=grp, s_=gsl: e.dma_start(
                out=w1s[:, s_ * 8192:s_ * 8192 + 8192].rearrange("p (kd h) -> p kd h", kd=16),
                in_=w1q.rearrange("(kd p) h -> p kd h", p=128)[:, :, g_ * 512:(g_ + 1) * 512]),
                deps=r_w1s[gsl].wdeps() + r_w1q.r, dma=True, stream=f"w1s{gsl}")
            tok_w2 = sc.step("scalar", lambda e, g_=grp, s_=gsl: e.dma_start(
                out=w2s[:, s_ * 8192:s_ * 8192 + 8192].rearrange("p (kd h) -> p kd h", kd=16),
                in_=w2q.rearrange("(kd p) h -> p kd h", p=128)[:, :, g_ * 512:(g_ + 1) * 512]),
                deps=r_w2s[gsl].wdeps() + r_w2q.r, dma=True, stream=f"w2s{gsl}")
            r_w1s[gsl].wrote(tok_w1)
            r_w2s[gsl].wrote(tok_w2)
            for sub in range(4):
                hc = grp * 4 + sub
                csl = hc % 2
                p1, p2 = px1l[csl], px2l[csl]

                def mk_x(buf, gsl_=gsl, sub_=sub, p_=None):
                    def f(te):
                        ins = None
                        for kd in range(16):
                            for n in range(2):
                                ins = te.matmul(p_[:, n * 512:(n + 1) * 512],
                                                buf[:, gsl_ * 8192 + kd * 512 + sub_ * 128:
                                                    gsl_ * 8192 + kd * 512 + sub_ * 128 + 128],
                                                ln2Tb[:, kd * NQ + n * 512: kd * NQ + n * 512 + 512],
                                                start=(kd == 0), stop=(kd == 15))
                        return ins
                    return f
                tok_x1m = sc.step("tensor", mk_x(w1s, p_=p1), deps=[tok_w1, tok_l2r] + r_px1[csl].wdeps())
                r_w1s[gsl].read(tok_x1m)
                r_px1[csl].wrote(tok_x1m)
                tok_x2m = sc.step("tensor", mk_x(w2s, p_=p2), deps=[tok_w2, tok_l2r] + r_px2[csl].wdeps())
                r_w2s[gsl].read(tok_x2m)
                r_px2[csl].wrote(tok_x2m)
                tok_sg = sc.step("scalar", lambda e, p_=p1: e.activation(sig[:], p_[:], AF.Sigmoid),
                                 deps=[tok_x1m] + r_sig.wdeps())
                r_px1[csl].read(tok_sg)
                r_sig.wrote(tok_sg)
                tok_h = sc.step("vector", lambda v, hc_=hc, p_=p2: v.tensor_tensor(
                    hid[:, hc_ * NQ:(hc_ + 1) * NQ], p_[:], sig[:], op=ALU.mult),
                    deps=[tok_x2m, tok_sg] + r_hid.rdeps() + ([r_hid.w] if r_hid.w else []))
                r_px2[csl].read(tok_h)
                r_sig.read(tok_h)
                r_hid.wrote(tok_h)
        sc.flush()
    hid_tok = r_hid.w

    sc.barrier()
    with (nc.sbuf_tensor("w3s", [128, 2 * 11264], BF16) as w3s,
          nc.sbuf_tensor("ffnT", [128, 16 * NQ], BF16) as ffnT,
          nc.sbuf_tensor("frm", [128, D], BF16) as frm,
          nc.sbuf_tensor("x1r", [128, 2 * D], F32) as x1r,
          nc.sbuf_tensor("outb", [128, D], F32) as outb,
          nc.psum_tensor("pf0", [128, 1024], F32) as pf0,
          nc.psum_tensor("pf1", [128, 1024], F32) as pf1,
          nc.psum_tensor("pT3", [128, 2048], BF16) as pT3):
        r_w3s = [Res(), Res()]
        r_ffnT = Res()
        r_frm = Res()
        r_x1r = [Res(), Res()]
        r_outb = Res()
        r_pf = [Res(), Res()]
        r_pT3 = Res()
        pfl = [pf0, pf1]
        out_toks = []

        for dg in range(8):
            gsl = dg % 2
            tok_ws = None
            tok_ws = sc.step("sync", lambda e, dg_=dg, s_=gsl: e.dma_start(
                out=w3s[:, s_ * 11264:(s_ + 1) * 11264].rearrange("p (hc d) -> p hc d", hc=44),
                in_=w3q.rearrange("(hc p) d -> p hc d", p=128)[:, :, dg_ * 256:(dg_ + 1) * 256]),
                deps=r_w3s[gsl].wdeps() + r_w3q.r, dma=True, stream=f"w3s{gsl}")
            tok_wsl = [tok_ws]
            r_w3s[gsl].wrote(tok_ws)
            for sub in range(2):
                dc = dg * 2 + sub
                csl = dc % 2
                pf = pfl[csl]

                def mk_f(sub_=sub, pf_=pf, gsl_=gsl):
                    def f(te):
                        ins = None
                        for hc in range(44):
                            for n in range(2):
                                ins = te.matmul(pf_[:, n * 512:(n + 1) * 512],
                                                w3s[:, gsl_ * 11264 + hc * 256 + sub_ * 128:
                                                    gsl_ * 11264 + hc * 256 + sub_ * 128 + 128],
                                                hid[:, hc * NQ + n * 512: hc * NQ + n * 512 + 512],
                                                start=(hc == 0), stop=(hc == 43))
                        return ins
                    return f
                tok_f = sc.step("tensor", mk_f(), deps=[tok_wsl[-1], hid_tok] + r_pf[csl].wdeps())
                r_w3s[gsl].read(tok_f)
                r_pf[csl].wrote(tok_f)
                tok_fe = sc.step("scalar", lambda e, dc_=dc, pf_=pf: e.activation(
                    ffnT[:, dc_ * NQ:(dc_ + 1) * NQ], pf_[:], AF.Identity),
                    deps=[tok_f] + r_ffnT.rdeps() + ([r_ffnT.w] if r_ffnT.w else []))
                r_pf[csl].read(tok_fe)
                r_ffnT.wrote(tok_fe)
        ffnT_tok = r_ffnT.w

        for qb in range(8):
            toks_tr = []
            for dc in range(16):
                def mk_tr3(qb_=qb, dc_=dc):
                    def f(te):
                        return te.transpose(pT3[:, dc_ * 128:(dc_ + 1) * 128],
                                            ffnT[:, dc_ * NQ + qb_ * 128: dc_ * NQ + qb_ * 128 + 128],
                                            ident[:])
                    return f
                dd = [ffnT_tok, t_id] + (r_pT3.wdeps() if dc == 0 else [])
                toks_tr.append(sc.step("tensor", mk_tr3(), deps=dd))
            r_pT3.wrote(toks_tr[-1])
            tok_fr = sc.step("vector", lambda v: v.tensor_copy(frm[:], pT3[:]),
                             deps=[toks_tr[-1]] + r_frm.wdeps())
            r_pT3.read(tok_fr)
            r_frm.wrote(tok_fr)
            sl = qb % 2
            co = 5 * sl
            rst = r_statl[sl]
            tok_s = ln_stats(sc, stat, sqs[:], rst, r_sqs, frm[:], tok_fr, D, co=co)
            xv = x1r[:, sl * D:(sl + 1) * D]
            tok_x1 = sc.step("scalar", lambda e, qb_=qb, x_=xv: e.dma_start(
                out=x_, in_=x1sp[qb_ * 128:(qb_ + 1) * 128, :]),
                deps=r_x1r[sl].wdeps() + [r_x1sp[qb].w], dma=True, stream=f"x1r{sl}")
            r_x1r[sl].wrote(tok_x1)
            tok_ln = sc.step("scalar", lambda e, c_=co: e.activation(sqs[:], frm[:], AF.Identity,
                                                                      bias=stat[:, c_ + 3:c_ + 4], scale=stat[:, c_ + 2:c_ + 3]),
                             deps=[tok_s] + r_sqs.wdeps())
            r_frm.read(tok_ln)
            rst.read(tok_ln)
            r_sqs.wrote(tok_ln)
            tok_o = sc.step("vector", lambda v, x_=xv: v.tensor_tensor(outb[:], x_, sqs[:], op=ALU.add),
                            deps=[tok_ln, tok_x1] + r_outb.wdeps())
            r_x1r[sl].read(tok_o)
            r_sqs.read(tok_o)
            r_outb.wrote(tok_o)
            tok_st = sc.step("gpsimd", lambda g, qb_=qb: g.dma_start(
                out=out[qb_ * 128:(qb_ + 1) * 128, :], in_=outb[:]),
                deps=[tok_o], dma=True, stream="out")
            r_outb.read(tok_st)
            out_toks.append(tok_st)

        sc.step("gpsimd", lambda g: g.engine_nop(), deps=out_toks)
        sc.flush()
    esD.close()
    es.close()
    return nc


_NC_CACHE = {}


def kernel(L0_attn_x, L0_attn_w_qkv, L0_attn_b_qkv, L0_attn_w_out, L0_attn_b_out,
           L0_ffn_w1, L0_ffn_w2, L0_ffn_w3):
    x = np.ascontiguousarray(L0_attn_x, dtype=np.float32)
    wqkv = np.ascontiguousarray(L0_attn_w_qkv, dtype=np.float32)
    bqkv = np.ascontiguousarray(L0_attn_b_qkv, dtype=np.float32)
    wout = np.ascontiguousarray(L0_attn_w_out, dtype=np.float32)
    bout = np.ascontiguousarray(L0_attn_b_out, dtype=np.float32)
    w1 = np.ascontiguousarray(L0_ffn_w1, dtype=np.float32)
    w2 = np.ascontiguousarray(L0_ffn_w2, dtype=np.float32)
    w3 = np.ascontiguousarray(L0_ffn_w3, dtype=np.float32)

    wqkv_t = np.ascontiguousarray(wqkv.reshape(16, 128, 3, 16, 128).transpose(3, 1, 2, 0, 4)
                                  .reshape(16, 128, 6144))
    bqkv_pm = np.ascontiguousarray(bqkv.reshape(48, 128).T)
    wout_t = np.ascontiguousarray(wout.reshape(16, 128, D))
    bout_v = np.ascontiguousarray(bout.reshape(1, D))
    w1_t = np.ascontiguousarray(w1.T).reshape(44, 128, D)
    w2_t = np.ascontiguousarray(w2.T).reshape(44, 128, D)
    w3_t = np.ascontiguousarray(w3.T).reshape(44, 128, D)
    identw = np.eye(128, dtype=ml_dtypes.bfloat16)

    if "nc" not in _NC_CACHE:
        _NC_CACHE["nc"] = build_nc()
    nc = _NC_CACHE["nc"]

    g1 = w1_t.reshape(88, 128, 1024)
    g2 = w2_t.reshape(88, 128, 1024)
    g3 = w3_t.reshape(88, 128, 1024)
    in_maps = []
    for c in range(8):
        b, half = c // 2, c % 2
        if half == 0:
            xbc = x[b]
        else:
            xbc = np.ascontiguousarray(np.concatenate([x[b, NQ:], x[b, :NQ]], axis=0))
        gsrc_c = np.ascontiguousarray(np.stack([g1[c * 11:(c + 1) * 11],
                                                g2[c * 11:(c + 1) * 11],
                                                g3[c * 11:(c + 1) * 11]]))
        in_maps.append({"xb": xbc, "wqkv_t": wqkv_t, "bqkv_pm": bqkv_pm, "wout_t": wout_t,
                        "bout_v": bout_v, "w1_t": w1_t, "w2_t": w2_t, "w3_t": w3_t,
                        "gsrc": gsrc_c, "identw": identw})
    res = run_bass_kernel_spmd(nc, in_maps, core_ids=list(range(8)))
    outp = np.empty((4, S, D), dtype=np.float32)
    for c in range(8):
        b, half = c // 2, c % 2
        outp[b, half * NQ:(half + 1) * NQ] = res.results[c]["out"]
    return outp


if __name__ == "__main__":
    import reference
    inp = reference.setup_inputs()
    inp = {k: np.asarray(v) for k, v in inp.items()}
    o = kernel(**inp)
    e = np.asarray(reference.reference(**inp))
    rv = float(((o - e) ** 2).mean() / (e ** 2).mean())
    print("resid_var_ratio:", rv, "maxabs:", float(np.abs(o - e).max()))
